# version 75
# baseline (speedup 1.0000x reference)
import numpy as np
import concourse.bass as bass
import concourse.mybir as mybir
from concourse.bass_utils import run_bass_kernel_spmd

F32 = mybir.dt.float32
BF16 = mybir.dt.bfloat16
AF = mybir.ActivationFunctionType
ALU = mybir.AluOpType

NCORES = 8
B_PER_CORE = 4
T = 2048
D = 1024
NJ = 8
CH = 64
TPG = 512
NGROUPS_FULL = T // 128

SAME_ENGINE_EDGES = True
NDMASLOT = 8


class Sched:
    def __init__(self, nc, es):
        self.nc = nc
        self.sems = {}
        self.counts = {}
        for e in ("tensor", "vector", "scalar", "gpsimd"):
            self.sems[e] = es.enter_context(nc.semaphore(f"sem_{e}"))
            self.counts[e] = 0
        for q in ("sync", "gpsimd", "scalar_q"):
            for i in range(NDMASLOT):
                k = ("q", q, i)
                self.sems[k] = es.enter_context(nc.semaphore(f"sem_{q}{i}"))
                self.counts[k] = 0
        self.dma_idx = {"sync": 0, "gpsimd": 0, "scalar_q": 0}
        self.ops = []
        self.writer = {}
        self.readers = {}

    def _record(self, eng, fn, reads, writes, sem_key, inc, extra_need=None):
        need = dict(extra_need or {})

        def add_dep(sk, cnt):
            if sk == sem_key and not SAME_ENGINE_EDGES:
                return
            if sk == sem_key and cnt >= self.counts[sem_key] + inc:
                return
            need[sk] = max(need.get(sk, 0), cnt)

        for k in reads:
            w = self.writer.get(k)
            if w:
                add_dep(*w)
        for k in writes:
            w = self.writer.get(k)
            if w:
                add_dep(*w)
            for r in self.readers.get(k, ()):
                add_dep(*r)
        after = self.counts[sem_key] + inc
        self.ops.append((eng, need, fn, sem_key, inc))
        self.counts[sem_key] = after
        for k in reads:
            self.readers.setdefault(k, []).append((sem_key, after))
        for k in writes:
            self.writer[k] = (sem_key, after)
            self.readers[k] = []
        return after

    def op(self, eng, fn, reads=(), writes=()):
        return self._record(eng, fn, reads, writes, eng, 1)

    def dma(self, q, fn, reads=(), writes=()):
        slot = self.dma_idx[q] % NDMASLOT
        self.dma_idx[q] += 1
        sk = ("q", q, slot)
        extra = {}
        if self.counts[sk] > 0:
            extra[sk] = self.counts[sk]
        return self._record(q, fn, reads, writes, sk, 16, extra)

    def emit(self, block, limit=None):
        all_keys = list(self.sems.keys())
        ops = self.ops if limit is None else self.ops[:limit]
        fincounts = {k: 0 for k in all_keys}
        for (e, need, fn, sem_key, inc) in ops:
            fincounts[sem_key] += inc

        qalias = {"scalar_q": "scalar"}

        def section(ename):
            def _(eng):
                hw = {k: 0 for k in all_keys}
                for (e, need, fn, sem_key, inc) in ops:
                    if qalias.get(e, e) != ename:
                        continue
                    for sk in all_keys:
                        cnt = need.get(sk)
                        if cnt is not None and cnt > hw[sk]:
                            eng.wait_ge(self.sems[sk], cnt)
                            hw[sk] = cnt
                    inst = fn(eng)
                    inst.then_inc(self.sems[sem_key], inc)
                if ename == "gpsimd":
                    for q in ("gpsimd", "sync", "scalar_q"):
                        for i in range(NDMASLOT):
                            k = ("q", q, i)
                            if fincounts[k] > 0:
                                eng.wait_ge(self.sems[k], fincounts[k])
            return _

        block.sync(section("sync"))
        block.vector(section("vector"))
        block.scalar(section("scalar"))
        block.tensor(section("tensor"))
        block.gpsimd(section("gpsimd"))


W_NAMES = ["wr", "wkf", "wvf", "wq", "wk", "wv", "wg", "wo"]


def MM(e, *a, **kw):
    kw.setdefault("skip_group_check", True)
    return e.matmul(*a, **kw)


def build_program(ngroups, limit=None):
    nc = bass.Bass("TRN2", target_bir_lowering=False, debug=False)
    d_x = nc.dram_tensor("x", [NGROUPS_FULL, NJ, 128, TPG], F32, kind="ExternalInput").ap()
    d_w = {n: nc.dram_tensor(n, [D, D], F32, kind="ExternalInput").ap() for n in W_NAMES}
    d_vecs = nc.dram_tensor("vecs", [128, 64], F32, kind="ExternalInput").ap()
    d_idf = nc.dram_tensor("idf", [128, 128], F32, kind="ExternalInput").ap()
    d_maskf = nc.dram_tensor("maskf", [128, 128], F32, kind="ExternalInput").ap()
    d_o = nc.dram_tensor("o", [NGROUPS_FULL, NJ, 128, TPG], F32, kind="ExternalOutput").ap()
    d_wg_s = nc.dram_tensor("wg_s", [128, NJ * D], BF16, kind="Internal").ap()
    d_wo_s = nc.dram_tensor("wo_s", [128, NJ * D], BF16, kind="Internal").ap()

    import contextlib
    es = contextlib.ExitStack()
    with es:
        sb = {}

        def alloc(name, shape, dtype):
            sb[name] = es.enter_context(nc.sbuf_tensor("s_" + name, shape, dtype))
            return sb[name]

        for n in W_NAMES:
            if n == "wo":
                continue
            alloc("W_" + n, [128, NJ * D], BF16)
        sb["W_wo"] = sb["W_wg"]
        vecs = alloc("vecs", [128, 64], F32)
        idf = alloc("idf", [128, 128], F32)
        mask = alloc("mask", [128, 128], BF16)
        onesneg = alloc("onesneg", [128, 1], BF16)
        onespos = alloc("onespos", [128, 1], BF16)
        ones1f = alloc("ones1f", [1, 128], F32)
        ones1b = alloc("ones1b", [1, 128], BF16)
        zeros64 = alloc("zeros64", [128, 64], BF16)
        epsb = alloc("epsb", [1, 1], F32)
        oneb = alloc("oneb", [128, 1], F32)
        wob = alloc("wob", [128, 8], F32)
        betab = alloc("betab", [128, 8], BF16)
        x_res = alloc("x_res", [128, 4096], F32)
        x2 = alloc("x2", [128, 4096], F32)
        xb = alloc("xb", [128, 4096], BF16)
        scA = alloc("scA", [128, 4096], BF16)
        scB = alloc("scB", [128, 4096], BF16)
        scE = alloc("scE", [128, 4096], BF16)
        VT = alloc("VT", [128, 4096], BF16)
        hst = alloc("hst", [128, 2048], BF16)
        dscr = alloc("dscr", [128, 1024], BF16)
        sgscr = alloc("sgscr", [128, 512], F32)
        lams = alloc("lams", [128, 512], F32)
        eps_ = alloc("eps", [128, 512], BF16)
        ems = alloc("ems", [128, 512], BF16)
        ktT = alloc("ktT", [128, 768], BF16)
        stS = alloc("stS", [128, 384], BF16)
        htmp = alloc("htmp", [128, 384], BF16)
        acs = alloc("acs", [128, 64], F32)
        c1 = alloc("c1", [128, 32], BF16)
        c3 = alloc("c3", [128, 32], BF16)
        lnb = alloc("lnb", [128, 1024], BF16)
        warm = alloc("warm", [1, 4], F32)

        P0 = es.enter_context(nc.psum_tensor("P0", [128, 512], F32))
        P1 = es.enter_context(nc.psum_tensor("P1", [128, 512], F32))
        P2 = es.enter_context(nc.psum_tensor("P2", [128, 512], F32))
        PA = [es.enter_context(nc.psum_tensor(f"PA{i}", [128, 512], F32)) for i in range(3)]
        PY = [es.enter_context(nc.psum_tensor(f"PY{i}", [128, 512], F32)) for i in range(2)]
        pp = [P0, P1, PA[0], PA[1], PA[2], PY[0], PY[1]]
        ppk = ["P0", "P1", "PA0", "PA1", "PA2", "PY0", "PY1"]

        s = Sched(nc, es)
        block = es.enter_context(nc.Block())

        s.op("vector", lambda e: e.memset(onesneg[:], -1.0 / 1024), writes=[("onesneg",)])
        s.op("vector", lambda e: e.memset(onespos[:], 1.0 / 1024), writes=[("onespos",)])
        s.op("vector", lambda e: e.memset(ones1f[:], 1.0), writes=[("ones1f",)])
        s.op("vector", lambda e: e.memset(ones1b[:], 1.0), writes=[("ones1b",)])
        s.op("vector", lambda e: e.memset(zeros64[:], 0.0), writes=[("zeros64",)])
        s.op("vector", lambda e: e.memset(epsb[:], 1e-5), writes=[("epsb",)])
        s.op("vector", lambda e: e.memset(oneb[:], 1.0), writes=[("oneb",)])
        s.op("vector", lambda e: e.memset(hst[:], 0.0),
             writes=[("h", j, b) for j in range(NJ) for b in range(4)])
        s.op("vector", lambda e: e.memset(c1[:], 0.0), writes=[("c1", j) for j in range(NJ)])
        s.op("vector", lambda e: e.memset(c3[:], 0.0), writes=[("c3", j) for j in range(NJ)])
        s.dma("sync", lambda e: e.dma_start(out=vecs[:, 0:32], in_=d_vecs[:, 0:32]),
              writes=[("vecs",)])
        s.dma("sync", lambda e: e.dma_start(out=idf[:], in_=d_idf[:]), writes=[("idf",)])
        s.dma("sync", lambda e: e.dma_start(out=x_res[:, 0:128], in_=d_maskf[:]),
              writes=[("x", 0, 0)])
        s.op("scalar", lambda e: e.copy(mask[:], x_res[:, 0:128]),
             reads=[("x", 0, 0)], writes=[("mask",)])
        s.op("scalar", lambda e: e.copy(betab[:], vecs[:, 24:32]),
             reads=[("vecs",)], writes=[("betab",)])
        s.op("vector",
             lambda e: e.tensor_scalar(vecs[:, 32:48], vecs[:, 0:16], 1.0, -1.0,
                                       ALU.mult, ALU.add),
             reads=[("vecs",)], writes=[("vecs",)])
        s.dma("gpsimd",
              lambda e: e.dma_start(
                  out=sb["W_wg"].rearrange("p (i e) -> p i e", i=NJ),
                  in_=d_w["wo"].rearrange("(i p) e -> p i e", p=128)),
              writes=[("W", "wg")])

        def fwob(e):
            for m in range(NJ):
                for k in range(NJ):
                    inst = MM(e, P2[0:128, 4 * m:4 * m + 1],
                              sb["W_wo"][:, D * k + 128 * m: D * k + 128 * m + 128],
                              betab[:, k:k + 1],
                              start=(k == 0), stop=(k == NJ - 1))
            return inst
        s.op("tensor", fwob, reads=[("W", "wg"), ("betab",)], writes=[("P2",)])
        s.op("scalar",
             lambda e: e.copy(wob[:],
                              P2[0:128, 0:32].rearrange("p (m c) -> p m c", c=4)[:, :, 0]),
             reads=[("P2",)], writes=[("wob",)])
        for k in range(NJ):
            s.op("vector",
                 lambda e, k=k: e.tensor_single_scalar(
                     sb["W_wo"][:, D * k:D * (k + 1)], sb["W_wo"][:, D * k:D * (k + 1)],
                     vecs[:, 16 + k:16 + k + 1], ALU.mult),
                 reads=[("W", "wg"), ("vecs",), ("wob",)], writes=[("W", "wg")])
        s.dma("scalar_q",
              lambda e: e.dma_start(out=d_wo_s, in_=sb["W_wo"][:]),
              reads=[("W", "wg")])
        for n in ["wkf", "wvf", "wr", "wq", "wk", "wv"]:
            dst, src = sb["W_" + n], d_w[n]
            s.dma("gpsimd",
                  lambda e, dst=dst, src=src: e.dma_start(
                      out=dst.rearrange("p (i e) -> p i e", i=NJ),
                      in_=src.rearrange("(i p) e -> p i e", p=128)),
                  writes=[("W", n)])
        s.dma("gpsimd",
              lambda e: e.dma_start(
                  out=sb["W_wg"].rearrange("p (i e) -> p i e", i=NJ),
                  in_=d_w["wg"].rearrange("(i p) e -> p i e", p=128)),
              writes=[("W", "wg")])

        xbufs = [x_res, x2]
        XC = [x_res]
        XN = [x2]
        XB = [0]

        mur_col = lambda j: vecs[:, j:j + 1]
        muk_col = lambda j: vecs[:, 8 + j:8 + j + 1]
        gam_col = lambda j: vecs[:, 16 + j:16 + j + 1]
        murm1_col = lambda j: vecs[:, 32 + j:32 + j + 1]
        mukm1_col = lambda j: vecs[:, 40 + j:40 + j + 1]

        pcnt = [0]
        TW = TPG

        WKEY = {"wo": "wg"}

        def wkey(wname):
            return ("W", WKEY.get(wname, wname))

        def proj(wname, src_name, src_t, evac, chase=False, mid=None, split=False,
                 mid_at=0):
            wt = sb["W_" + wname]
            if split:
                nsplit = NJ - 1 if split == "nop2" else NJ
                banks = []
                for m in range(nsplit - 1):
                    pi = pcnt[0] % len(pp)
                    pcnt[0] += 1
                    banks.append((pp[pi], ppk[pi]))
                if split == "nop2":
                    pi = pcnt[0] % len(pp)
                    pcnt[0] += 1
                    banks.append((pp[pi], ppk[pi]))
                else:
                    banks.append((P2, "P2"))
                KH = NJ // 2

                def half(m, lo, P):
                    def fn(e, m=m, lo=lo, P=P):
                        for k in range(lo, lo + KH):
                            inst = MM(e, P[:, 0:TW],
                                      wt[:, D * k + 128 * m: D * k + 128 * m + 128],
                                      src_t[:, TW * k: TW * k + TW],
                                      start=(k == 0), stop=(k == NJ - 1))
                        return inst
                    return fn
                P0_, Pk0 = banks[0]
                for k in range(KH):
                    s.op("tensor",
                         lambda e, k=k, P=P0_: MM(
                             e, P[:, 0:TW],
                             wt[:, D * k: D * k + 128],
                             src_t[:, TW * k: TW * k + TW],
                             start=(k == 0), stop=False),
                         reads=[wkey(wname), (src_name, k)], writes=[(Pk0,)])
                for m in range(1, nsplit):
                    P, Pk = banks[m]
                    s.op("tensor", half(m, 0, P),
                         reads=[wkey(wname)] + [(src_name, k) for k in range(KH)],
                         writes=[(Pk,)])
                for k in range(KH, NJ):
                    s.op("tensor",
                         lambda e, k=k, P=P0_: MM(
                             e, P[:, 0:TW],
                             wt[:, D * k: D * k + 128],
                             src_t[:, TW * k: TW * k + TW],
                             start=False, stop=(k == NJ - 1)),
                         reads=[wkey(wname), (src_name, k)], writes=[(Pk0,)])
                evac(0, P0_[:, 0:TW], Pk0)
                for m in range(1, nsplit):
                    P, Pk = banks[m]
                    s.op("tensor", half(m, KH, P),
                         reads=[wkey(wname)] + [(src_name, k) for k in range(KH, NJ)],
                         writes=[(Pk,)])
                    evac(m, P[:, 0:TW], Pk)
                for m in range(nsplit, NJ):
                    P, Pk = banks[m - 1]

                    def full(e, m=m, P=P):
                        for k in range(NJ):
                            inst = MM(e, P[:, 0:TW],
                                      wt[:, D * k + 128 * m: D * k + 128 * m + 128],
                                      src_t[:, TW * k: TW * k + TW],
                                      start=(k == 0), stop=(k == NJ - 1))
                        return inst
                    s.op("tensor", full,
                         reads=[wkey(wname)] + [(src_name, k) for k in range(NJ)],
                         writes=[(Pk,)])
                    evac(m, P[:, 0:TW], Pk)
                return
            for m in range(NJ):
                pi = pcnt[0] % len(pp)
                pcnt[0] += 1
                P, Pk = pp[pi], ppk[pi]
                if chase and m == 0:
                    for k in range(NJ):
                        s.op("tensor",
                             lambda e, k=k, P=P, wt=wt, src_t=src_t: MM(
                                 e, P[:, 0:TW],
                                 wt[:, D * k: D * k + 128],
                                 src_t[:, TW * k: TW * k + TW],
                                 start=(k == 0), stop=(k == NJ - 1)),
                             reads=[wkey(wname), (src_name, k)], writes=[(Pk,)])
                else:
                    def fn(e, m=m, P=P, wt=wt, src_t=src_t):
                        for k in range(NJ):
                            inst = MM(e, P[:, 0:TW],
                                      wt[:, D * k + 128 * m: D * k + 128 * m + 128],
                                      src_t[:, TW * k: TW * k + TW],
                                      start=(k == 0), stop=(k == NJ - 1))
                        return inst
                    s.op("tensor", fn,
                         reads=[wkey(wname)] + [(src_name, k) for k in range(NJ)],
                         writes=[(Pk,)])
                evac(m, P[:, 0:TW], Pk)
                if m == mid_at and mid is not None:
                    mid()

        def ln_sq_j(j, src_keys, src_t, eng="gpsimd", blk=None):
            b = j if blk is None else blk
            sq_ap = VT[:, TW * b: TW * b + TW]
            s.op(eng if eng in ("gpsimd", "vector") else "vector",
                 lambda e, j=j, sq_ap=sq_ap, src_t=src_t: e.tensor_tensor(
                     sq_ap, src_t[:, TW * j:TW * j + TW],
                     src_t[:, TW * j:TW * j + TW], ALU.mult),
                 reads=src_keys, writes=[("VTb", b)])

        def ln_mm_j(j, src_name, src_t, blk=None):
            b = j if blk is None else blk
            def fnmm(e, j=j, b=b, src_t=src_t):
                MM(e, P2[0:1, 0:TW], onesneg[:],
                   src_t[:, TW * j:TW * j + TW],
                   start=(j == 0), stop=False)
                return MM(e, P2[64:65, 0:TW], onespos[:],
                          VT[:, TW * b:TW * b + TW],
                          start=(j == 0), stop=(j == NJ - 1),
                          tile_position=(0, 64))
            s.op("tensor", fnmm,
                 reads=[(src_name, j), ("VTb", b), ("onesneg",), ("onespos",)],
                 writes=[("P2",)])

        def ln_finish(src_name, src_t, done_mms=0, late_rs=False):
            for j in range(done_mms, NJ):
                ln_mm_j(j, src_name, src_t)
            s.op("scalar", lambda e: e.copy(ems[0:1, 0:512], P2[0:1, 0:512]),
                 reads=[("P2",)], writes=[("em", 0)])

            def fb1(e):
                return MM(e, PA[2][:, 0:512], ones1b[:], ems[0:1, 0:512],
                          start=True, stop=True)
            s.op("tensor", fb1, reads=[("em", 0), ("ones1b",)], writes=[("PA2",)])
            s.op("scalar", lambda e: e.copy(lnb[:, 0:512], PA[2][:, 0:512]),
                 reads=[("PA2",)], writes=[("lnb_nm",)])
            s.op("scalar", lambda e: e.activation(sgscr[0:1, 0:512], P2[0:1, 0:512],
                                                  AF.Square),
                 reads=[("P2",)], writes=[("stB",)])
            s.op("vector", lambda e: e.tensor_tensor(sgscr[0:1, 0:512], P2[64:65, 0:512],
                                                     sgscr[0:1, 0:512], ALU.subtract),
                 reads=[("P2",), ("stB",)], writes=[("stB",)])
            s.op("scalar", lambda e: e.activation(sgscr[0:1, 0:512], sgscr[0:1, 0:512],
                                                  AF.Ln, bias=epsb[:]),
                 reads=[("stB",), ("epsb",)], writes=[("stB",)])
            s.op("scalar", lambda e: e.activation(eps_[0:1, 0:512], sgscr[0:1, 0:512],
                                                  AF.Exp, scale=-0.5),
                 reads=[("stB",)], writes=[("ep", 0)])

            def emit_rs():
                def fb2(e):
                    return MM(e, P2[:, 0:512], ones1b[:], eps_[0:1, 0:512],
                              start=True, stop=True)
                s.op("tensor", fb2, reads=[("ep", 0), ("ones1b",), ("P2",)],
                     writes=[("P2",)])
                s.op("scalar", lambda e: e.copy(lnb[:, 512:1024], P2[:, 0:512]),
                     reads=[("P2",)], writes=[("lnb_rs",)])
            if late_rs:
                return emit_rs
            emit_rs()

        def ln_apply_x(j):
            u_ap = scE[:, TW * j:TW * j + TW]
            s.op("gpsimd",
                 lambda e, j=j, u_ap=u_ap: e.tensor_tensor(
                     u_ap, xb[:, TW * j:TW * j + TW], lnb[:, 0:512], ALU.add),
                 reads=[("xb", j), ("lnb_nm",)], writes=[("u", j), ("uu", j)])
            s.op("vector",
                 lambda e, j=j, u_ap=u_ap: e.tensor_tensor(
                     u_ap, u_ap, lnb[:, 512:1024], ALU.mult),
                 reads=[("u", j), ("lnb_rs",)], writes=[("u", j)])

        def ln_apply_y(j):
            t_ap = scE[:, TW * j:TW * j + TW]
            s.op("vector",
                 lambda e, j=j, t_ap=t_ap: e.tensor_tensor(
                     t_ap, t_ap, lnb[:, 0:512], ALU.add),
                 reads=[("u", j), ("lnb_nm",)], writes=[("u", j)])
            s.op("vector",
                 lambda e, j=j, t_ap=t_ap: e.tensor_tensor(
                     t_ap, t_ap, lnb[:, 512:1024], ALU.mult),
                 reads=[("u", j), ("lnb_rs",)], writes=[("u", j)])

        def shadow_j(j, src_t=None, bkey=None):
            st = XC[0] if src_t is None else src_t
            bk = XB[0] if bkey is None else bkey
            s.op("scalar",
                 lambda e, j=j, st=st: e.copy(xb[:, TW * j:TW * j + TW],
                                              st[:, TW * j:TW * j + TW]),
                 reads=[("x", bk, j)], writes=[("xb", j)])

        def ffn(carry_t, carry_key, tail_stats, tail_io=None):
            xc, xn, xbk = XC[0], XN[0], XB[0]
            nbk = 1 - xbk
            io_next = tail_io is not None and tail_io + 1 < ngroups

            def d_shift(j, u_ap, d_ap):
                s.op("vector",
                     lambda e, d_ap=d_ap, u_ap=u_ap: e.tensor_tensor(
                         d_ap[:, 1:512], u_ap[:, 1:512], u_ap[:, 0:511], ALU.subtract),
                     reads=[("uu", j)], writes=[("dscr", j % 2)])
                def fx0(e, j=j, d_ap=d_ap, u_ap=u_ap):
                    dv = d_ap.rearrange("p (x t) -> p x t", t=CH)[:, 0:4, 0]
                    uv = u_ap.rearrange("p (x t) -> p x t", t=CH)[:, 0:4, 0]
                    return e.tensor_tensor(dv, uv, carry_t[:, 4 * j:4 * j + 4],
                                           ALU.subtract)
                s.op("vector", fx0, reads=[("uu", j), (carry_key, j)],
                     writes=[("dscr", j % 2)])
                def fx1(e, j=j, d_ap=d_ap, u_ap=u_ap):
                    dv = d_ap.rearrange("p (x t) -> p x t", t=CH)[:, 4:8, 0]
                    uv = u_ap.rearrange("p (x t) -> p x t", t=CH)[:, 4:8, 0]
                    up = u_ap.rearrange("p (x t) -> p x t", t=CH)[:, 0:4, CH - 1]
                    return e.tensor_tensor(dv, uv, up, ALU.subtract)
                s.op("vector", fx1, reads=[("uu", j)], writes=[("dscr", j % 2)])

            for j in range(NJ):
                d_ap = dscr[:, 512 * (j % 2): 512 * (j % 2) + 512]
                u_ap = scE[:, TW * j:TW * j + TW]
                s.op("gpsimd",
                     lambda e, j=j, u_ap=u_ap: e.tensor_tensor(
                         u_ap, xb[:, TW * j:TW * j + TW], lnb[:, 0:512], ALU.add),
                     reads=[("xb", j), ("lnb_nm",)], writes=[("uu", j), ("u", j)])
                s.op("vector",
                     lambda e, j=j, u_ap=u_ap: e.tensor_tensor(
                         u_ap, u_ap, lnb[:, 512:1024], ALU.mult),
                     reads=[("uu", j), ("lnb_rs",)], writes=[("uu", j)])
                d_shift(j, u_ap, d_ap)
                s.op("vector",
                     lambda e, j=j, d_ap=d_ap, u_ap=u_ap: e.scalar_tensor_tensor(
                         scB[:, TW * j:TW * j + TW], d_ap, mukm1_col(j), u_ap,
                         ALU.mult, ALU.add),
                     reads=[("dscr", j % 2), ("uu", j), ("vecs",)], writes=[("scB", j)])

            def evac_k(m, pap, pkey):
                kr_ap = dscr[:, 512 * (m % 2): 512 * (m % 2) + 512]
                s.op("scalar",
                     lambda e, pap=pap, kr_ap=kr_ap: e.activation(kr_ap, pap, AF.Relu),
                     reads=[(pkey,)], writes=[("dscr", m % 2)])
                s.op("vector",
                     lambda e, m=m, kr_ap=kr_ap: e.tensor_tensor(
                         VT[:, TW * m:TW * m + TW], kr_ap, kr_ap, ALU.mult),
                     reads=[("dscr", m % 2)], writes=[("VTb", m)])
                if io_next:
                    shadow_j(m, src_t=xn, bkey=nbk)
            proj("wkf", "scB", scB, evac_k, split=True)

            for j in range(NJ):
                d_ap = dscr[:, 512 * (j % 2): 512 * (j % 2) + 512]
                u_ap = scE[:, TW * j:TW * j + TW]
                d_shift(j, u_ap, d_ap)
                s.op("vector",
                     lambda e, j=j, d_ap=d_ap, u_ap=u_ap: e.scalar_tensor_tensor(
                         scA[:, TW * j:TW * j + TW], d_ap, murm1_col(j), u_ap,
                         ALU.mult, ALU.add),
                     reads=[("dscr", j % 2), ("uu", j), ("vecs",)], writes=[("scA", j)])

            def evac_v(m, pap, pkey):
                s.op("scalar",
                     lambda e, m=m, pap=pap: e.copy(
                         scB[:, TW * m:TW * m + TW], pap),
                     reads=[(pkey,)], writes=[("scB", m)])
                if m == 5:
                    s.op("scalar",
                         lambda e: e.activation(warm[0:1, 0:1], oneb[0:1, 0:1],
                                                AF.Sigmoid),
                         reads=[("oneb",)], writes=[("warm",)])
            proj("wvf", "VTb", VT, evac_v, chase=True)
            for j in range(NJ):
                def fc(e, j=j):
                    uv = scE[:, TW * j:TW * j + TW].rearrange(
                        "p (x t) -> p x t", t=CH)[:, 4:8, CH - 1]
                    return e.copy(carry_t[:, 4 * j:4 * j + 4], uv)
                s.op("scalar", fc, reads=[("uu", j)], writes=[(carry_key, j)])

            def evac_r(m, pap, pkey):
                t_ap = dscr[:, 512 * (m % 2): 512 * (m % 2) + 512]
                s.op("scalar",
                     lambda e, pap=pap, t_ap=t_ap: e.activation(t_ap, pap, AF.Sigmoid),
                     reads=[(pkey,)], writes=[("dscr", m % 2)])
                s.op("vector",
                     lambda e, m=m, t_ap=t_ap: e.tensor_tensor(
                         t_ap, t_ap, scB[:, TW * m:TW * m + TW], ALU.mult),
                     reads=[("dscr", m % 2), ("scB", m)], writes=[("dscr", m % 2)])
                s.op("vector",
                     lambda e, m=m, t_ap=t_ap, xc=xc: e.scalar_tensor_tensor(
                         xc[:, TW * m:TW * m + TW], t_ap, 0.5,
                         xc[:, TW * m:TW * m + TW], ALU.mult, ALU.add),
                     reads=[("dscr", m % 2), ("x", xbk, m)], writes=[("x", xbk, m)])
                if tail_stats:
                    if m >= 1:
                        shadow_j(m - 1)
                        ln_sq_j(m - 1, [("x", xbk, m - 1)], xc, eng="gpsimd")
                    if m >= 3:
                        ln_mm_j(m - 3, "xb", xb)
                if tail_io is not None:
                    g_ = tail_io
                    if io_next:
                        ln_sq_j(m, [("x", nbk, m)], xn, eng="vector")
                        if m >= 2:
                            ln_mm_j(m - 2, "xb", xb)
                    else:
                        s.dma("gpsimd" if m % 2 == 0 else "scalar_q",
                              lambda e, m=m, g_=g_, xc=xc: e.dma_start(
                                  out=d_o[g_, m, :, :],
                                  in_=xc[:, TW * m:TW * m + TW]),
                              reads=[("x", xbk, m)])
            proj("wr", "scA", scA, evac_r)
            s.op("scalar",
                 lambda e: e.activation(warm[0:1, 0:1], oneb[0:1, 0:1], AF.Ln),
                 reads=[("oneb",)], writes=[("warm",)])
            if tail_stats:
                shadow_j(NJ - 1)
                ln_sq_j(NJ - 1, [("x", xbk, NJ - 1)], xc)
                ln_mm_j(NJ - 3, "xb", xb)
            if io_next:
                for m in range(NJ - 2, NJ):
                    ln_mm_j(m, "xb", xb)

        for g in range(ngroups):
            XC[0] = xbufs[g % 2]
            XN[0] = xbufs[(g + 1) % 2]
            XB[0] = g % 2
            if g == 0:
                for h in range(2):
                    s.dma("sync",
                          lambda e, h=h, g=g: e.dma_start(
                              out=x_res[:, 2048 * h:2048 * h + 2048]
                              .rearrange("p (j w) -> p j w", j=4),
                              in_=d_x[g, 4 * h:4 * h + 4, :, :].rearrange("j p w -> p j w")),
                          writes=[("x", 0, jj2) for jj2 in range(4 * h, 4 * h + 4)])
                s.dma("sync",
                      lambda e: e.dma_start(out=d_wg_s, in_=sb["W_wg"][:]),
                      reads=[("W", "wg")])
                for j in range(NJ):
                    shadow_j(j)
                    ln_sq_j(j, [("x", 0, j)], x_res, eng="vector")
                    if j >= 2:
                        ln_mm_j(j - 2, "xb", xb)
                done1 = NJ - 2
            else:
                done1 = NJ

            ln_finish("xb", xb, done_mms=done1)
            ffn(c1, "c1", tail_stats=True)

            if g > 0:
                pb2 = (g - 1) % 2
                for h in range(2):
                    s.dma("gpsimd" if h == 0 else "scalar_q",
                          lambda e, h=h, g=g, pb2=pb2: e.dma_start(
                              out=d_o[g - 1, 4 * h:4 * h + 4, :, :]
                              .rearrange("j p w -> p j w"),
                              in_=xbufs[pb2][:, 2048 * h:2048 * h + 2048]
                              .rearrange("p (j w) -> p j w", j=4)),
                          reads=[("x", pb2, jj2) for jj2 in range(4 * h, 4 * h + 4)])
            if g + 1 < ngroups:
                nb2 = (g + 1) % 2
                for h in range(2):
                    s.dma("sync",
                          lambda e, h=h, g=g, nb2=nb2: e.dma_start(
                              out=xbufs[nb2][:, 2048 * h:2048 * h + 2048]
                              .rearrange("p (j w) -> p j w", j=4),
                              in_=d_x[g + 1, 4 * h:4 * h + 4, :, :]
                              .rearrange("j p w -> p j w")),
                          writes=[("x", nb2, jj2) for jj2 in range(4 * h, 4 * h + 4)])
            emit_rs2 = ln_finish("xb", xb, done_mms=NJ - 2, late_rs=True)
            for j in range(NJ):
                u_ap = scE[:, TW * j:TW * j + TW]
                s.op("vector",
                     lambda e, j=j, u_ap=u_ap: e.tensor_tensor(
                         u_ap, xb[:, TW * j:TW * j + TW], lnb[:, 0:512], ALU.add),
                     reads=[("xb", j), ("lnb_nm",)], writes=[("u", j), ("uu", j)])
                s.op("vector",
                     lambda e, j=j, u_ap=u_ap: e.tensor_tensor(
                         xb[:, TW * j:TW * j + TW], u_ap, lnb[:, 512:1024], ALU.mult),
                     reads=[("uu", j), ("lnb_rs",)], writes=[("xb", j)])

            def evac_q(m, pap, pkey):
                s.op("scalar", lambda e, m=m, pap=pap: e.copy(
                    scA[:, TW * m:TW * m + TW], pap),
                    reads=[(pkey,)], writes=[("scA", m)])
            proj("wq", "u", scE, evac_q, chase=True, mid=emit_rs2, mid_at=1)

            def evac_kk(m, pap, pkey):
                s.op("scalar", lambda e, m=m, pap=pap: e.copy(
                    scB[:, TW * m:TW * m + TW], pap),
                    reads=[(pkey,)], writes=[("scB", m)])
            proj("wk", "xb", xb, evac_kk)

            def evac_g(m, pap, pkey):
                j = m
                sg_ap = sgscr[:, 0:512]
                loga_ap = sgscr[:, 0:512]
                lam_ap = lams[:, 0:512]
                ep_ap = eps_[:, 0:512]
                em_ap = ems[:, 0:512]
                s.op("scalar", lambda e, pap=pap, sg_ap=sg_ap: e.activation(
                    sg_ap, pap, AF.Exp, scale=-1.0),
                    reads=[(pkey,)], writes=[("sgs", 0)])
                s.op("scalar",
                     lambda e, sg_ap=sg_ap, loga_ap=loga_ap: e.activation(
                         loga_ap, sg_ap, AF.Ln, bias=oneb[:]),
                     reads=[("sgs", 0), ("oneb",)], writes=[("loga", 0)])
                for cb in range(8):
                    s.op("vector",
                         lambda e, cb=cb, loga_ap=loga_ap, lam_ap=lam_ap:
                         e.tensor_tensor_scan(
                             lam_ap[:, CH * cb:CH * cb + CH],
                             loga_ap[:, CH * cb:CH * cb + CH],
                             zeros64[:], 0.0, ALU.add, ALU.add),
                         reads=[("loga", 0), ("zeros64",)], writes=[("lam", cb)])
                lam_keys = [("lam", cb) for cb in range(8)]
                s.op("scalar",
                     lambda e, lam_ap=lam_ap, ep_ap=ep_ap: e.activation(
                         ep_ap, lam_ap, AF.Exp, scale=-1.0),
                     reads=lam_keys, writes=[("ep", 0), ("ep_r", 0), ("ep_r", 1)])
                s.op("vector",
                     lambda e, ep_ap=ep_ap: e.tensor_tensor(
                         ep_ap, ep_ap, lnb[:, 512:1024], ALU.mult),
                     reads=[("ep", 0), ("lnb_rs",)], writes=[("ep", 0)])
                s.op("scalar",
                     lambda e, lam_ap=lam_ap, em_ap=em_ap: e.activation(
                         em_ap, lam_ap, AF.Exp),
                     reads=lam_keys, writes=[("em", 0)])

                def fac(e, j=j, lam_ap=lam_ap):
                    lsrc = lam_ap.rearrange("p (c b t) -> p c b t", c=2, b=4)[:, :, :, CH - 1]
                    dv = acs[:, 8 * j:8 * j + 8].rearrange("p (b c) -> p b c", b=4)
                    return e.activation(dv, lsrc.rearrange("p c b -> p b c"), AF.Exp,
                                        scale=-1.0)
                s.op("scalar", fac, reads=lam_keys,
                     writes=[("ac", j)])
                s.op("vector",
                     lambda e, j=j, ep_ap=ep_ap: e.tensor_tensor(
                         scA[:, TW * j:TW * j + TW], scA[:, TW * j:TW * j + TW],
                         ep_ap, ALU.mult),
                     reads=[("scA", j), ("ep", 0)], writes=[("scA", j)])
                s.op("vector",
                     lambda e, j=j, em_ap=em_ap: e.tensor_tensor(
                         scB[:, TW * j:TW * j + TW], scB[:, TW * j:TW * j + TW],
                         em_ap, ALU.mult),
                     reads=[("scB", j), ("em", 0)], writes=[("scB", j)])

            def g_mtile(m):
                pi = pcnt[0] % len(pp)
                pcnt[0] += 1
                P, Pk = pp[pi], ppk[pi]
                wt = sb["W_wg"]

                def fn(e, m=m, P=P, wt=wt):
                    for k in range(NJ):
                        inst = MM(e, P[:, 0:TW],
                                  wt[:, D * k + 128 * m: D * k + 128 * m + 128],
                                  xb[:, TW * k: TW * k + TW],
                                  start=(k == 0), stop=(k == NJ - 1))
                    return inst
                s.op("tensor", fn,
                     reads=[("W", "wg")] + [("xb", k) for k in range(NJ)],
                     writes=[(Pk,)])
                evac_g(m, P[:, 0:TW], Pk)

            def vt_fill(bp, c, n):
                pi = pcnt[0] % len(pp)
                pcnt[0] += 1
                P, Pk = pp[pi], ppk[pi]

                def fvt(e, bp=bp, c=c, n=n, P=P):
                    wt = sb["W_wv"]
                    for k in range(NJ):
                        inst = MM(e,
                            P[0:128, 0:512],
                            xb[:, TW * k + 256 * c + 128 * bp:
                                TW * k + 256 * c + 128 * bp + 128],
                            wt[:, D * k + 512 * n: D * k + 512 * n + 512],
                            start=(k == 0), stop=(k == NJ - 1))
                    return inst
                s.op("tensor", fvt,
                     reads=[("W", "wv")] + [("xb", k) for k in range(NJ)],
                     writes=[(Pk,)])
                s.op("vector",
                     lambda e, bp=bp, c=c, n=n, P=P: e.tensor_copy(
                         VT[:, 2048 * bp + 1024 * c + 512 * n:
                            2048 * bp + 1024 * c + 512 * n + 512],
                         P[:, 0:512]),
                     reads=[(Pk,)], writes=[("VTb", 4 * bp + 2 * c + n)])

            vt_order = [(bp, c, n) for bp in range(2) for c in range(2) for n in range(2)]
            for m in range(NJ):
                g_mtile(m)
                vt_fill(*vt_order[m])
            s.dma("scalar_q",
                  lambda e: e.dma_start(out=sb["W_wg"][:], in_=d_wo_s),
                  writes=[("W", "wg")])

            units = [(c, j, bp) for c in range(2) for j in range(NJ) for bp in range(2)]

            kslot = {}
            kcnt = [0]

            def emit_ktT(c, j):
                jj = kcnt[0] % 3
                kcnt[0] += 1
                kslot[(c, j)] = jj
                for bp in range(2):
                    s.dma("sync",
                          lambda e, c=c, j=j, bp=bp, jj=jj: e.dma_start(
                              out=ktT[:, 256 * jj + 128 * bp: 256 * jj + 128 * bp + 128],
                              in_=scB[:, TW * j + 256 * c + 128 * bp:
                                      TW * j + 256 * c + 128 * bp + 128],
                              transpose=True),
                          reads=[("scB", j)], writes=[("ktT", jj, bp)])

            PAlist = [PA[0], PA[1], PA[2], P0]
            PAkeys = ["PA0", "PA1", "PA2", "P0"]
            PYlist = [PY[0], PY[1], P1]
            PYkeys = ["PY0", "PY1", "P1"]

            def emit_front(n):
                c, j, bp = units[n]
                un = n % 4
                us_ = n % 3
                PAu = PAlist[un]
                PAk = PAkeys[un]
                st_ap = stS[:, 128 * us_:128 * us_ + 128]
                base = TW * j + 256 * c

                def fS(hh):
                    def _f(e, j=j, bp=bp, c=c, PAu=PAu, hh=hh, base=base):
                        for b01 in range(2):
                            b = 2 * bp + b01
                            inst = MM(e,
                                PAu[64 * b01:64 * b01 + 64, 64 * hh:64 * hh + 64],
                                scB[64 * hh:64 * hh + 64, base + CH * b: base + CH * b + CH],
                                scA[64 * hh:64 * hh + 64, base + CH * b: base + CH * b + CH],
                                start=(hh == 0), stop=(hh == 1),
                                tile_position=(64 * hh, 64 * b01)
                                if (hh or b01) else None)
                        return inst
                    return _f
                s.op("tensor", fS(0), reads=[("scB", j), ("scA", j)], writes=[(PAk,)])
                s.op("tensor", fS(1), reads=[("scB", j), ("scA", j)], writes=[(PAk,)])
                s.op("vector",
                     lambda e, st_ap=st_ap, PAu=PAu: e.tensor_tensor(
                         st_ap, PAu[:, 0:128], mask[:], ALU.mult),
                     reads=[(PAk,), ("mask",)], writes=[("st", us_)])

            def emit_back(n):
                c, j, bp = units[n]
                jj = kslot[(c, j)]
                un = n % 4
                us_ = n % 3
                uy = n % 3
                PAu, PYu = PAlist[un], PYlist[uy]
                PAk, PYk = PAkeys[un], PYkeys[uy]
                st_ap = stS[:, 128 * us_:128 * us_ + 128]
                base = TW * j + 256 * c
                vbase = 2048 * bp + 1024 * c

                def fY_intra(b01):
                    def _f(e, j=j, bp=bp, PYu=PYu, st_ap=st_ap, b01=b01, vbase=vbase):
                        for hh in range(2):
                            inst = MM(e,
                                PYu[64 * hh:64 * hh + 64, 64 * b01:64 * b01 + 64],
                                VT[64 * b01:64 * b01 + 64,
                                   vbase + 64 * (2 * j + hh): vbase + 64 * (2 * j + hh) + 64],
                                st_ap[64 * b01:64 * b01 + 64, 64 * hh:64 * hh + 64],
                                start=(b01 == 0), stop=False,
                                tile_position=(64 * b01, 64 * hh)
                                if (b01 or hh) else None)
                        return inst
                    return _f

                def fY_inter(hh):
                    def _f(e, j=j, bp=bp, PYu=PYu, hh=hh, base=base):
                        for b01 in range(2):
                            b = 2 * bp + b01
                            inst = MM(e,
                                PYu[64 * hh:64 * hh + 64, 64 * b01:64 * b01 + 64],
                                hst[64 * hh:64 * hh + 64,
                                    64 * (4 * j + b): 64 * (4 * j + b) + 64],
                                scA[64 * hh:64 * hh + 64, base + CH * b: base + CH * b + CH],
                                start=False, stop=(b01 == 1),
                                tile_position=(64 * hh, 64 * hh) if hh else None)
                        return inst
                    return _f
                yreads = [("st", us_), ("VTb", 4 * bp + 2 * c + j // 4), ("scA", j),
                          ("h", j, 2 * bp), ("h", j, 2 * bp + 1)]
                s.op("tensor", fY_intra(0), reads=yreads, writes=[(PYk,)])
                s.op("tensor", fY_intra(1), reads=yreads, writes=[(PYk,)])
                s.op("tensor", fY_inter(0), reads=yreads, writes=[(PYk,)])
                s.op("tensor", fY_inter(1), reads=yreads, writes=[(PYk,)])

                def fH(b01):
                    def _f(e, j=j, bp=bp, PAu=PAu, b01=b01, jj=jj, vbase=vbase):
                        for hh in range(2):
                            inst = MM(e,
                                PAu[64 * hh:64 * hh + 64,
                                    128 + 64 * b01: 128 + 64 * b01 + 64],
                                ktT[64 * b01:64 * b01 + 64,
                                    256 * jj + 128 * bp + 64 * hh:
                                    256 * jj + 128 * bp + 64 * hh + 64],
                                VT[64 * b01:64 * b01 + 64,
                                   vbase + 64 * (2 * j + hh): vbase + 64 * (2 * j + hh) + 64],
                                start=(b01 == 0), stop=(b01 == 1),
                                tile_position=(64 * b01, 64 * hh)
                                if (b01 or hh) else None)
                        return inst
                    return _f
                hreads = [("ktT", jj, bp), ("VTb", 4 * bp + 2 * c + j // 4), ("st", us_)]
                s.op("tensor", fH(0), reads=hreads, writes=[(PAk,)])
                s.op("tensor", fH(1), reads=hreads, writes=[(PAk,)])
                ht_ap = htmp[:, 128 * us_:128 * us_ + 128]
                s.op("vector",
                     lambda e, j=j, bp=bp, PAu=PAu, ht_ap=ht_ap: e.tensor_tensor(
                         ht_ap,
                         hst[:, 64 * (4 * j + 2 * bp): 64 * (4 * j + 2 * bp) + 128],
                         PAu[:, 128:256], ALU.add),
                     reads=[(PAk,), ("h", j, 2 * bp), ("h", j, 2 * bp + 1)],
                     writes=[("htmp", us_)])
                for b01 in range(2):
                    b = 2 * bp + b01
                    acc = 8 * j + 2 * b + c
                    s.op("vector",
                         lambda e, j=j, b=b, b01=b01, acc=acc, ht_ap=ht_ap:
                         e.tensor_single_scalar(
                             hst[:, 64 * (4 * j + b): 64 * (4 * j + b) + 64],
                             ht_ap[:, 64 * b01:64 * b01 + 64],
                             acs[:, acc:acc + 1], ALU.mult),
                         reads=[("htmp", us_), ("ac", j)], writes=[("h", j, b)])
                s.op("scalar",
                     lambda e, j=j, bp=bp, base=base, PYu=PYu: e.copy(
                         scE[:, base + 128 * bp: base + 128 * bp + 128],
                         PYu[:, 0:128]),
                     reads=[(PYk,)], writes=[("u", j)])

            emit_ktT(0, 0)
            emit_ktT(0, 1)
            emit_ktT(0, 2)
            emit_front(0)
            emit_front(1)
            emit_front(2)
            YMAP = [0, 4, 1, 5, 2, 6, 3, 7]

            def yln_sq_chase(nb):
                if nb >= 17 and (nb - 17) % 2 == 0:
                    jy = (nb - 17) // 2
                    if jy <= 5:
                        ln_sq_j(jy, [("u", jy)], scE, eng="gpsimd", blk=YMAP[jy])

            for n in range(3, len(units)):
                emit_back(n - 3)
                yln_sq_chase(n - 3)
                if n % 2 == 0:
                    cc, jn, _ = units[n]
                    jn2 = jn + 1
                    cn = cc
                    if jn2 >= NJ:
                        jn2 -= NJ
                        cn += 1
                    if cn < 2:
                        emit_ktT(cn, jn2)
                emit_front(n)
            for nb in (len(units) - 3, len(units) - 2, len(units) - 1):
                emit_back(nb)
                yln_sq_chase(nb)

            for jy in (6, 7):
                ln_sq_j(jy, [("u", jy)], scE, eng=("gpsimd" if jy % 2 else "vector"),
                        blk=YMAP[jy])
            for j in range(NJ):
                ln_mm_j(j, "u", scE, blk=YMAP[j])
            ln_finish("u", scE, done_mms=NJ)
            for j in range(NJ):
                ln_apply_y(j)

            xcw, xbw = XC[0], XB[0]

            def evac_o(m, pap, pkey):
                s.op("vector",
                     lambda e, m=m, pap=pap, xcw=xcw: e.scalar_tensor_tensor(
                         xcw[:, TW * m:TW * m + TW], pap, wob[:, m:m + 1],
                         xcw[:, TW * m:TW * m + TW], ALU.add, ALU.add),
                     reads=[(pkey,), ("x", xbw, m), ("wob",)],
                     writes=[("x", xbw, m)])
                if m >= 1:
                    shadow_j(m - 1)
                    ln_sq_j(m - 1, [("x", xbw, m - 1)], xcw, eng="gpsimd")
                if m >= 3:
                    ln_mm_j(m - 3, "xb", xb)
            proj("wo", "u", scE, evac_o, chase=True)
            if g + 1 < ngroups:
                s.dma("scalar_q",
                      lambda e: e.dma_start(out=sb["W_wg"][:], in_=d_wg_s),
                      writes=[("W", "wg")])
            shadow_j(NJ - 1)
            ln_sq_j(NJ - 1, [("x", xbw, NJ - 1)], xcw)
            ln_mm_j(NJ - 3, "xb", xb)

            ln_finish("xb", xb, done_mms=NJ - 2)
            ffn(c3, "c3", tail_stats=False, tail_io=g)

        s.emit(block, limit=limit)
    return nc


def make_host_consts():
    idf = np.eye(128, dtype=np.float32)
    m64 = np.triu(np.ones((CH, CH), dtype=np.float32))
    maskf = np.block([[m64, m64], [m64, m64]]).astype(np.float32)
    return idf, maskf


def host_pre(xc):
    v = xc.reshape(B_PER_CORE, NGROUPS_FULL, 2, CH, NJ, 128)
    v = v.transpose(1, 4, 5, 2, 0, 3)
    return np.ascontiguousarray(v.reshape(NGROUPS_FULL, NJ, 128, TPG))


def host_post(oc):
    v = oc.reshape(NGROUPS_FULL, NJ, 128, 2, B_PER_CORE, CH)
    v = v.transpose(4, 0, 3, 5, 1, 2)
    return np.ascontiguousarray(v.reshape(B_PER_CORE, T, D))


def make_in_maps(x, wT, vecs):
    idf, maskf = make_host_consts()
    in_maps = []
    for c in range(NCORES):
        m = {"x": host_pre(x[B_PER_CORE * c:B_PER_CORE * (c + 1)]),
             "vecs": vecs, "idf": idf, "maskf": maskf}
        for n in W_NAMES:
            m[n] = wT[n]
        in_maps.append(m)
    return in_maps


def prep_inputs(L0_attn_x, L0_attn_Wq, L0_attn_Wk, L0_attn_Wv, L0_attn_Wg, L0_attn_Wo,
                L0_attn_gamma, L0_attn_beta, L0_ffn_W_r, L0_ffn_W_k, L0_ffn_W_v,
                L0_ffn_mu_r, L0_ffn_mu_k):
    x = np.ascontiguousarray(np.asarray(L0_attn_x, dtype=np.float32))
    wmap = {
        "wr": L0_ffn_W_r, "wkf": L0_ffn_W_k, "wvf": L0_ffn_W_v,
        "wq": L0_attn_Wq, "wk": L0_attn_Wk, "wv": L0_attn_Wv,
        "wg": L0_attn_Wg, "wo": L0_attn_Wo,
    }
    wT = {n: np.ascontiguousarray(np.asarray(w, np.float32).T) for n, w in wmap.items()}
    vecs = np.zeros((128, 64), dtype=np.float32)
    vecs[:, 0:8] = np.asarray(L0_ffn_mu_r, np.float32).reshape(8, 128).T
    vecs[:, 8:16] = np.asarray(L0_ffn_mu_k, np.float32).reshape(8, 128).T
    vecs[:, 16:24] = np.asarray(L0_attn_gamma, np.float32).reshape(8, 128).T
    vecs[:, 24:32] = np.asarray(L0_attn_beta, np.float32).reshape(8, 128).T
    return x, wT, vecs


def kernel(**inputs):
    x, wT, vecs = prep_inputs(**inputs)
    nc = build_program(NGROUPS_FULL)
    in_maps = make_in_maps(x, wT, vecs)
    res = run_bass_kernel_spmd(nc, in_maps, core_ids=list(range(NCORES)))
    return np.concatenate([host_post(np.asarray(r["o"], np.float32))
                           for r in res.results], axis=0)





# version 76
# speedup vs baseline: 1.2035x; 1.2035x over previous
import numpy as np
import concourse.bass as bass
import concourse.mybir as mybir
from concourse.bass_utils import run_bass_kernel_spmd

F32 = mybir.dt.float32
BF16 = mybir.dt.bfloat16
AF = mybir.ActivationFunctionType
ALU = mybir.AluOpType

NCORES = 8
B_PER_CORE = 4
T = 2048
D = 1024
NJ = 8
CH = 64
TPG = 512
NGROUPS_FULL = T // 128

SAME_ENGINE_EDGES = True
NDMASLOT = 8


class Sched:
    def __init__(self, nc, es):
        self.nc = nc
        self.sems = {}
        self.counts = {}
        for e in ("tensor", "vector", "scalar", "gpsimd"):
            self.sems[e] = es.enter_context(nc.semaphore(f"sem_{e}"))
            self.counts[e] = 0
        for q in ("sync", "gpsimd", "scalar_q"):
            for i in range(NDMASLOT):
                k = ("q", q, i)
                self.sems[k] = es.enter_context(nc.semaphore(f"sem_{q}{i}"))
                self.counts[k] = 0
        self.dma_idx = {"sync": 0, "gpsimd": 0, "scalar_q": 0}
        self.ops = []
        self.writer = {}
        self.readers = {}

    def _record(self, eng, fn, reads, writes, sem_key, inc, extra_need=None):
        need = dict(extra_need or {})

        def add_dep(sk, cnt):
            if sk == sem_key and not SAME_ENGINE_EDGES:
                return
            if sk == sem_key and cnt >= self.counts[sem_key] + inc:
                return
            need[sk] = max(need.get(sk, 0), cnt)

        for k in reads:
            w = self.writer.get(k)
            if w:
                add_dep(*w)
        for k in writes:
            w = self.writer.get(k)
            if w:
                add_dep(*w)
            for r in self.readers.get(k, ()):
                add_dep(*r)
        after = self.counts[sem_key] + inc
        self.ops.append((eng, need, fn, sem_key, inc))
        self.counts[sem_key] = after
        for k in reads:
            self.readers.setdefault(k, []).append((sem_key, after))
        for k in writes:
            self.writer[k] = (sem_key, after)
            self.readers[k] = []
        return after

    def op(self, eng, fn, reads=(), writes=()):
        return self._record(eng, fn, reads, writes, eng, 1)

    def dma(self, q, fn, reads=(), writes=()):
        slot = self.dma_idx[q] % NDMASLOT
        self.dma_idx[q] += 1
        sk = ("q", q, slot)
        extra = {}
        if self.counts[sk] > 0:
            extra[sk] = self.counts[sk]
        return self._record(q, fn, reads, writes, sk, 16, extra)

    def emit(self, block, limit=None):
        all_keys = list(self.sems.keys())
        ops = self.ops if limit is None else self.ops[:limit]
        fincounts = {k: 0 for k in all_keys}
        for (e, need, fn, sem_key, inc) in ops:
            fincounts[sem_key] += inc

        qalias = {"scalar_q": "scalar"}

        def section(ename):
            def _(eng):
                hw = {k: 0 for k in all_keys}
                for (e, need, fn, sem_key, inc) in ops:
                    if qalias.get(e, e) != ename:
                        continue
                    for sk in all_keys:
                        cnt = need.get(sk)
                        if cnt is not None and cnt > hw[sk]:
                            eng.wait_ge(self.sems[sk], cnt)
                            hw[sk] = cnt
                    inst = fn(eng)
                    inst.then_inc(self.sems[sem_key], inc)
                if ename == "gpsimd":
                    for q in ("gpsimd", "sync", "scalar_q"):
                        for i in range(NDMASLOT):
                            k = ("q", q, i)
                            if fincounts[k] > 0:
                                eng.wait_ge(self.sems[k], fincounts[k])
            return _

        block.sync(section("sync"))
        block.vector(section("vector"))
        block.scalar(section("scalar"))
        block.tensor(section("tensor"))
        block.gpsimd(section("gpsimd"))


W_NAMES = ["wr", "wkf", "wvf", "wq", "wk", "wv", "wg", "wo"]


def MM(e, *a, **kw):
    kw.setdefault("skip_group_check", True)
    return e.matmul(*a, **kw)


def build_program(ngroups, limit=None):
    nc = bass.Bass("TRN2", target_bir_lowering=False, debug=False)
    d_x = nc.dram_tensor("x", [NGROUPS_FULL, NJ, 128, TPG], F32, kind="ExternalInput").ap()
    d_w = {n: nc.dram_tensor(n, [D, D], F32, kind="ExternalInput").ap() for n in W_NAMES}
    d_vecs = nc.dram_tensor("vecs", [128, 64], F32, kind="ExternalInput").ap()
    d_idf = nc.dram_tensor("idf", [128, 128], F32, kind="ExternalInput").ap()
    d_maskf = nc.dram_tensor("maskf", [128, 128], F32, kind="ExternalInput").ap()
    d_o = nc.dram_tensor("o", [NGROUPS_FULL, NJ, 128, TPG], F32, kind="ExternalOutput").ap()
    d_wg_s = nc.dram_tensor("wg_s", [128, NJ * D], BF16, kind="Internal").ap()
    d_wo_s = nc.dram_tensor("wo_s", [128, NJ * D], BF16, kind="Internal").ap()

    import contextlib
    es = contextlib.ExitStack()
    with es:
        sb = {}

        def alloc(name, shape, dtype):
            sb[name] = es.enter_context(nc.sbuf_tensor("s_" + name, shape, dtype))
            return sb[name]

        for n in W_NAMES:
            if n == "wo":
                continue
            alloc("W_" + n, [128, NJ * D], BF16)
        sb["W_wo"] = sb["W_wg"]
        vecs = alloc("vecs", [128, 64], F32)
        idf = alloc("idf", [128, 128], F32)
        mask = alloc("mask", [128, 128], BF16)
        onesneg = alloc("onesneg", [128, 1], BF16)
        onespos = alloc("onespos", [128, 1], BF16)
        ones1f = alloc("ones1f", [1, 128], F32)
        ones1b = alloc("ones1b", [1, 128], BF16)
        zeros64 = alloc("zeros64", [128, 64], BF16)
        epsb = alloc("epsb", [1, 1], F32)
        oneb = alloc("oneb", [128, 1], F32)
        wob = alloc("wob", [128, 8], F32)
        betab = alloc("betab", [128, 8], BF16)
        x_res = alloc("x_res", [128, 4096], F32)
        x2 = alloc("x2", [128, 4096], F32)
        xb = alloc("xb", [128, 4096], BF16)
        scA = alloc("scA", [128, 4096], BF16)
        scB = alloc("scB", [128, 4096], BF16)
        scE = alloc("scE", [128, 4096], BF16)
        VT = alloc("VT", [128, 4096], BF16)
        hst = alloc("hst", [128, 2048], BF16)
        dscr = alloc("dscr", [128, 1024], BF16)
        sgscr = alloc("sgscr", [128, 512], F32)
        lams = alloc("lams", [128, 512], F32)
        eps_ = alloc("eps", [128, 512], BF16)
        ems = alloc("ems", [128, 512], BF16)
        ktT = alloc("ktT", [128, 768], BF16)
        stS = alloc("stS", [128, 384], BF16)
        htmp = alloc("htmp", [128, 384], BF16)
        acs = alloc("acs", [128, 64], F32)
        c1 = alloc("c1", [128, 32], BF16)
        c3 = alloc("c3", [128, 32], BF16)
        lnb = alloc("lnb", [128, 1024], BF16)
        warm = alloc("warm", [1, 4], F32)

        P0 = es.enter_context(nc.psum_tensor("P0", [128, 512], F32))
        P1 = es.enter_context(nc.psum_tensor("P1", [128, 512], F32))
        P2 = es.enter_context(nc.psum_tensor("P2", [128, 512], F32))
        PA = [es.enter_context(nc.psum_tensor(f"PA{i}", [128, 512], F32)) for i in range(3)]
        PY = [es.enter_context(nc.psum_tensor(f"PY{i}", [128, 512], F32)) for i in range(2)]
        pp = [P0, P1, PA[0], PA[1], PA[2], PY[0], PY[1]]
        ppk = ["P0", "P1", "PA0", "PA1", "PA2", "PY0", "PY1"]

        s = Sched(nc, es)
        block = es.enter_context(nc.Block())

        s.op("vector", lambda e: e.memset(onesneg[:], -1.0 / 1024), writes=[("onesneg",)])
        s.op("vector", lambda e: e.memset(onespos[:], 1.0 / 1024), writes=[("onespos",)])
        s.op("vector", lambda e: e.memset(ones1f[:], 1.0), writes=[("ones1f",)])
        s.op("vector", lambda e: e.memset(ones1b[:], 1.0), writes=[("ones1b",)])
        s.op("vector", lambda e: e.memset(zeros64[:], 0.0), writes=[("zeros64",)])
        s.op("vector", lambda e: e.memset(epsb[:], 1e-5), writes=[("epsb",)])
        s.op("vector", lambda e: e.memset(oneb[:], 1.0), writes=[("oneb",)])
        s.op("vector", lambda e: e.memset(hst[:], 0.0),
             writes=[("h", j, b) for j in range(NJ) for b in range(4)])
        s.op("vector", lambda e: e.memset(c1[:], 0.0), writes=[("c1", j) for j in range(NJ)])
        s.op("vector", lambda e: e.memset(c3[:], 0.0), writes=[("c3", j) for j in range(NJ)])
        s.dma("sync", lambda e: e.dma_start(out=vecs[:, 0:32], in_=d_vecs[:, 0:32]),
              writes=[("vecs",)])
        s.dma("sync", lambda e: e.dma_start(out=idf[:], in_=d_idf[:]), writes=[("idf",)])
        s.dma("sync", lambda e: e.dma_start(out=x_res[:, 0:128], in_=d_maskf[:]),
              writes=[("x", 0, 0)])
        s.op("scalar", lambda e: e.copy(mask[:], x_res[:, 0:128]),
             reads=[("x", 0, 0)], writes=[("mask",)])
        s.op("scalar", lambda e: e.copy(betab[:], vecs[:, 24:32]),
             reads=[("vecs",)], writes=[("betab",)])
        s.op("vector",
             lambda e: e.tensor_scalar(vecs[:, 32:48], vecs[:, 0:16], 1.0, -1.0,
                                       ALU.mult, ALU.add),
             reads=[("vecs",)], writes=[("vecs",)])
        s.dma("gpsimd",
              lambda e: e.dma_start(
                  out=sb["W_wg"].rearrange("p (i e) -> p i e", i=NJ),
                  in_=d_w["wo"].rearrange("(i p) e -> p i e", p=128)),
              writes=[("W", "wg")])

        def fwob(e):
            for m in range(NJ):
                for k in range(NJ):
                    inst = MM(e, P2[0:128, 4 * m:4 * m + 1],
                              sb["W_wo"][:, D * k + 128 * m: D * k + 128 * m + 128],
                              betab[:, k:k + 1],
                              start=(k == 0), stop=(k == NJ - 1))
            return inst
        s.op("tensor", fwob, reads=[("W", "wg"), ("betab",)], writes=[("P2",)])
        s.op("scalar",
             lambda e: e.copy(wob[:],
                              P2[0:128, 0:32].rearrange("p (m c) -> p m c", c=4)[:, :, 0]),
             reads=[("P2",)], writes=[("wob",)])
        for k in range(NJ):
            s.op("vector",
                 lambda e, k=k: e.tensor_single_scalar(
                     sb["W_wo"][:, D * k:D * (k + 1)], sb["W_wo"][:, D * k:D * (k + 1)],
                     vecs[:, 16 + k:16 + k + 1], ALU.mult),
                 reads=[("W", "wg"), ("vecs",), ("wob",)], writes=[("W", "wg")])
        s.dma("scalar_q",
              lambda e: e.dma_start(out=d_wo_s, in_=sb["W_wo"][:]),
              reads=[("W", "wg")])
        for n in ["wkf", "wvf", "wr", "wq", "wk", "wv"]:
            dst, src = sb["W_" + n], d_w[n]
            s.dma("gpsimd",
                  lambda e, dst=dst, src=src: e.dma_start(
                      out=dst.rearrange("p (i e) -> p i e", i=NJ),
                      in_=src.rearrange("(i p) e -> p i e", p=128)),
                  writes=[("W", n)])
        s.dma("gpsimd",
              lambda e: e.dma_start(
                  out=sb["W_wg"].rearrange("p (i e) -> p i e", i=NJ),
                  in_=d_w["wg"].rearrange("(i p) e -> p i e", p=128)),
              writes=[("W", "wg")])

        xbufs = [x_res, x2]
        XC = [x_res]
        XN = [x2]
        XB = [0]

        mur_col = lambda j: vecs[:, j:j + 1]
        muk_col = lambda j: vecs[:, 8 + j:8 + j + 1]
        gam_col = lambda j: vecs[:, 16 + j:16 + j + 1]
        murm1_col = lambda j: vecs[:, 32 + j:32 + j + 1]
        mukm1_col = lambda j: vecs[:, 40 + j:40 + j + 1]

        pcnt = [0]
        TW = TPG

        WKEY = {"wo": "wg"}

        def wkey(wname):
            return ("W", WKEY.get(wname, wname))

        def proj(wname, src_name, src_t, evac, chase=False, mid=None, split=False,
                 mid_at=0, mid_pre=False):
            wt = sb["W_" + wname]
            if split:
                nsplit = NJ - 1 if split == "nop2" else NJ
                banks = []
                for m in range(nsplit - 1):
                    pi = pcnt[0] % len(pp)
                    pcnt[0] += 1
                    banks.append((pp[pi], ppk[pi]))
                if split == "nop2":
                    pi = pcnt[0] % len(pp)
                    pcnt[0] += 1
                    banks.append((pp[pi], ppk[pi]))
                else:
                    banks.append((P2, "P2"))
                KH = NJ // 2

                def half(m, lo, P):
                    def fn(e, m=m, lo=lo, P=P):
                        for k in range(lo, lo + KH):
                            inst = MM(e, P[:, 0:TW],
                                      wt[:, D * k + 128 * m: D * k + 128 * m + 128],
                                      src_t[:, TW * k: TW * k + TW],
                                      start=(k == 0), stop=(k == NJ - 1))
                        return inst
                    return fn
                P0_, Pk0 = banks[0]
                for k in range(KH):
                    s.op("tensor",
                         lambda e, k=k, P=P0_: MM(
                             e, P[:, 0:TW],
                             wt[:, D * k: D * k + 128],
                             src_t[:, TW * k: TW * k + TW],
                             start=(k == 0), stop=False),
                         reads=[wkey(wname), (src_name, k)], writes=[(Pk0,)])
                for m in range(1, nsplit):
                    P, Pk = banks[m]
                    s.op("tensor", half(m, 0, P),
                         reads=[wkey(wname)] + [(src_name, k) for k in range(KH)],
                         writes=[(Pk,)])
                for k in range(KH, NJ):
                    s.op("tensor",
                         lambda e, k=k, P=P0_: MM(
                             e, P[:, 0:TW],
                             wt[:, D * k: D * k + 128],
                             src_t[:, TW * k: TW * k + TW],
                             start=False, stop=(k == NJ - 1)),
                         reads=[wkey(wname), (src_name, k)], writes=[(Pk0,)])
                evac(0, P0_[:, 0:TW], Pk0)
                for m in range(1, nsplit):
                    P, Pk = banks[m]
                    s.op("tensor", half(m, KH, P),
                         reads=[wkey(wname)] + [(src_name, k) for k in range(KH, NJ)],
                         writes=[(Pk,)])
                    evac(m, P[:, 0:TW], Pk)
                for m in range(nsplit, NJ):
                    P, Pk = banks[m - 1]

                    def full(e, m=m, P=P):
                        for k in range(NJ):
                            inst = MM(e, P[:, 0:TW],
                                      wt[:, D * k + 128 * m: D * k + 128 * m + 128],
                                      src_t[:, TW * k: TW * k + TW],
                                      start=(k == 0), stop=(k == NJ - 1))
                        return inst
                    s.op("tensor", full,
                         reads=[wkey(wname)] + [(src_name, k) for k in range(NJ)],
                         writes=[(Pk,)])
                    evac(m, P[:, 0:TW], Pk)
                return
            for m in range(NJ):
                pi = pcnt[0] % len(pp)
                pcnt[0] += 1
                P, Pk = pp[pi], ppk[pi]
                if chase and m == 0:
                    for k in range(NJ):
                        s.op("tensor",
                             lambda e, k=k, P=P, wt=wt, src_t=src_t: MM(
                                 e, P[:, 0:TW],
                                 wt[:, D * k: D * k + 128],
                                 src_t[:, TW * k: TW * k + TW],
                                 start=(k == 0), stop=(k == NJ - 1)),
                             reads=[wkey(wname), (src_name, k)], writes=[(Pk,)])
                else:
                    def fn(e, m=m, P=P, wt=wt, src_t=src_t):
                        for k in range(NJ):
                            inst = MM(e, P[:, 0:TW],
                                      wt[:, D * k + 128 * m: D * k + 128 * m + 128],
                                      src_t[:, TW * k: TW * k + TW],
                                      start=(k == 0), stop=(k == NJ - 1))
                        return inst
                    s.op("tensor", fn,
                         reads=[wkey(wname)] + [(src_name, k) for k in range(NJ)],
                         writes=[(Pk,)])
                if m == mid_at and mid is not None and mid_pre:
                    mid()
                evac(m, P[:, 0:TW], Pk)
                if m == mid_at and mid is not None and not mid_pre:
                    mid()

        def ln_sq_j(j, src_keys, src_t, eng="gpsimd", blk=None):
            b = j if blk is None else blk
            sq_ap = VT[:, TW * b: TW * b + TW]
            s.op(eng if eng in ("gpsimd", "vector") else "vector",
                 lambda e, j=j, sq_ap=sq_ap, src_t=src_t: e.tensor_tensor(
                     sq_ap, src_t[:, TW * j:TW * j + TW],
                     src_t[:, TW * j:TW * j + TW], ALU.mult),
                 reads=src_keys, writes=[("VTb", b)])

        def ln_mm_j(j, src_name, src_t, blk=None):
            b = j if blk is None else blk
            def fnmm(e, j=j, b=b, src_t=src_t):
                MM(e, P2[0:1, 0:TW], onesneg[:],
                   src_t[:, TW * j:TW * j + TW],
                   start=(j == 0), stop=False)
                return MM(e, P2[64:65, 0:TW], onespos[:],
                          VT[:, TW * b:TW * b + TW],
                          start=(j == 0), stop=(j == NJ - 1),
                          tile_position=(0, 64))
            s.op("tensor", fnmm,
                 reads=[(src_name, j), ("VTb", b), ("onesneg",), ("onespos",)],
                 writes=[("P2",)])

        def ln_finish(src_name, src_t, done_mms=0, late_rs=False):
            for j in range(done_mms, NJ):
                ln_mm_j(j, src_name, src_t)
            s.op("scalar", lambda e: e.copy(ems[0:1, 0:512], P2[0:1, 0:512]),
                 reads=[("P2",)], writes=[("em", 0)])

            def fb1(e):
                return MM(e, PA[2][:, 0:512], ones1b[:], ems[0:1, 0:512],
                          start=True, stop=True)
            s.op("tensor", fb1, reads=[("em", 0), ("ones1b",)], writes=[("PA2",)])
            s.op("scalar", lambda e: e.copy(lnb[:, 0:512], PA[2][:, 0:512]),
                 reads=[("PA2",)], writes=[("lnb_nm",)])
            s.op("scalar", lambda e: e.activation(sgscr[0:1, 0:512], P2[0:1, 0:512],
                                                  AF.Square),
                 reads=[("P2",)], writes=[("stB",)])
            s.op("vector", lambda e: e.tensor_tensor(sgscr[0:1, 0:512], P2[64:65, 0:512],
                                                     sgscr[0:1, 0:512], ALU.subtract),
                 reads=[("P2",), ("stB",)], writes=[("stB",)])
            s.op("scalar", lambda e: e.activation(sgscr[0:1, 0:512], sgscr[0:1, 0:512],
                                                  AF.Ln, bias=epsb[:]),
                 reads=[("stB",), ("epsb",)], writes=[("stB",)])
            s.op("scalar", lambda e: e.activation(eps_[0:1, 0:512], sgscr[0:1, 0:512],
                                                  AF.Exp, scale=-0.5),
                 reads=[("stB",)], writes=[("ep", 0)])

            def emit_rs():
                def fb2(e):
                    return MM(e, P2[:, 0:512], ones1b[:], eps_[0:1, 0:512],
                              start=True, stop=True)
                s.op("tensor", fb2, reads=[("ep", 0), ("ones1b",), ("P2",)],
                     writes=[("P2",)])
                s.op("scalar", lambda e: e.copy(lnb[:, 512:1024], P2[:, 0:512]),
                     reads=[("P2",)], writes=[("lnb_rs",)])
            if late_rs:
                return emit_rs
            emit_rs()

        def ln_apply_x(j):
            u_ap = scE[:, TW * j:TW * j + TW]
            s.op("gpsimd",
                 lambda e, j=j, u_ap=u_ap: e.tensor_tensor(
                     u_ap, xb[:, TW * j:TW * j + TW], lnb[:, 0:512], ALU.add),
                 reads=[("xb", j), ("lnb_nm",)], writes=[("u", j), ("uu", j)])
            s.op("vector",
                 lambda e, j=j, u_ap=u_ap: e.tensor_tensor(
                     u_ap, u_ap, lnb[:, 512:1024], ALU.mult),
                 reads=[("u", j), ("lnb_rs",)], writes=[("u", j)])

        def ln_apply_y(j):
            t_ap = scE[:, TW * j:TW * j + TW]
            s.op("vector",
                 lambda e, j=j, t_ap=t_ap: e.tensor_tensor(
                     t_ap, t_ap, lnb[:, 0:512], ALU.add),
                 reads=[("u", j), ("lnb_nm",)], writes=[("u", j)])

        def shadow_j(j, src_t=None, bkey=None):
            st = XC[0] if src_t is None else src_t
            bk = XB[0] if bkey is None else bkey
            s.op("scalar",
                 lambda e, j=j, st=st: e.copy(xb[:, TW * j:TW * j + TW],
                                              st[:, TW * j:TW * j + TW]),
                 reads=[("x", bk, j)], writes=[("xb", j)])

        def ffn(carry_t, carry_key, tail_stats, tail_io=None):
            xc, xn, xbk = XC[0], XN[0], XB[0]
            nbk = 1 - xbk
            io_next = tail_io is not None and tail_io + 1 < ngroups

            def d_shift(j, u_ap, d_ap):
                s.op("vector",
                     lambda e, d_ap=d_ap, u_ap=u_ap: e.tensor_tensor(
                         d_ap[:, 1:512], u_ap[:, 1:512], u_ap[:, 0:511], ALU.subtract),
                     reads=[("uu", j)], writes=[("dscr", j % 2)])
                def fx0(e, j=j, d_ap=d_ap, u_ap=u_ap):
                    dv = d_ap.rearrange("p (x t) -> p x t", t=CH)[:, 0:4, 0]
                    uv = u_ap.rearrange("p (x t) -> p x t", t=CH)[:, 0:4, 0]
                    return e.tensor_tensor(dv, uv, carry_t[:, 4 * j:4 * j + 4],
                                           ALU.subtract)
                s.op("vector", fx0, reads=[("uu", j), (carry_key, j)],
                     writes=[("dscr", j % 2)])
                def fx1(e, j=j, d_ap=d_ap, u_ap=u_ap):
                    dv = d_ap.rearrange("p (x t) -> p x t", t=CH)[:, 4:8, 0]
                    uv = u_ap.rearrange("p (x t) -> p x t", t=CH)[:, 4:8, 0]
                    up = u_ap.rearrange("p (x t) -> p x t", t=CH)[:, 0:4, CH - 1]
                    return e.tensor_tensor(dv, uv, up, ALU.subtract)
                s.op("vector", fx1, reads=[("uu", j)], writes=[("dscr", j % 2)])

            for j in range(NJ):
                d_ap = dscr[:, 512 * (j % 2): 512 * (j % 2) + 512]
                u_ap = scE[:, TW * j:TW * j + TW]
                s.op("gpsimd",
                     lambda e, j=j, u_ap=u_ap: e.tensor_tensor(
                         u_ap, xb[:, TW * j:TW * j + TW], lnb[:, 0:512], ALU.add),
                     reads=[("xb", j), ("lnb_nm",)], writes=[("uu", j), ("u", j)])
                s.op("vector",
                     lambda e, j=j, u_ap=u_ap: e.tensor_tensor(
                         u_ap, u_ap, lnb[:, 512:1024], ALU.mult),
                     reads=[("uu", j), ("lnb_rs",)], writes=[("uu", j)])
                d_shift(j, u_ap, d_ap)
                s.op("vector",
                     lambda e, j=j, d_ap=d_ap, u_ap=u_ap: e.scalar_tensor_tensor(
                         scB[:, TW * j:TW * j + TW], d_ap, mukm1_col(j), u_ap,
                         ALU.mult, ALU.add),
                     reads=[("dscr", j % 2), ("uu", j), ("vecs",)], writes=[("scB", j)])

            def evac_k(m, pap, pkey):
                kr_ap = dscr[:, 512 * (m % 2): 512 * (m % 2) + 512]
                s.op("scalar",
                     lambda e, pap=pap, kr_ap=kr_ap: e.activation(kr_ap, pap, AF.Relu),
                     reads=[(pkey,)], writes=[("dscr", m % 2)])
                s.op("vector",
                     lambda e, m=m, kr_ap=kr_ap: e.tensor_tensor(
                         VT[:, TW * m:TW * m + TW], kr_ap, kr_ap, ALU.mult),
                     reads=[("dscr", m % 2)], writes=[("VTb", m)])
                if io_next:
                    shadow_j(m, src_t=xn, bkey=nbk)
            proj("wkf", "scB", scB, evac_k, split=True)

            for j in range(NJ):
                d_ap = dscr[:, 512 * (j % 2): 512 * (j % 2) + 512]
                u_ap = scE[:, TW * j:TW * j + TW]
                d_shift(j, u_ap, d_ap)
                s.op("vector",
                     lambda e, j=j, d_ap=d_ap, u_ap=u_ap: e.scalar_tensor_tensor(
                         scA[:, TW * j:TW * j + TW], d_ap, murm1_col(j), u_ap,
                         ALU.mult, ALU.add),
                     reads=[("dscr", j % 2), ("uu", j), ("vecs",)], writes=[("scA", j)])

            def evac_v(m, pap, pkey):
                s.op("scalar",
                     lambda e, m=m, pap=pap: e.copy(
                         scB[:, TW * m:TW * m + TW], pap),
                     reads=[(pkey,)], writes=[("scB", m)])
                if m == 5:
                    s.op("scalar",
                         lambda e: e.activation(warm[0:1, 0:1], oneb[0:1, 0:1],
                                                AF.Sigmoid),
                         reads=[("oneb",)], writes=[("warm",)])
            proj("wvf", "VTb", VT, evac_v, chase=True)
            for j in range(NJ):
                def fc(e, j=j):
                    uv = scE[:, TW * j:TW * j + TW].rearrange(
                        "p (x t) -> p x t", t=CH)[:, 4:8, CH - 1]
                    return e.copy(carry_t[:, 4 * j:4 * j + 4], uv)
                s.op("scalar", fc, reads=[("uu", j)], writes=[(carry_key, j)])

            def evac_r(m, pap, pkey):
                t_ap = dscr[:, 512 * (m % 2): 512 * (m % 2) + 512]
                s.op("scalar",
                     lambda e, pap=pap, t_ap=t_ap: e.activation(t_ap, pap, AF.Sigmoid),
                     reads=[(pkey,)], writes=[("dscr", m % 2)])
                s.op("vector",
                     lambda e, m=m, t_ap=t_ap: e.tensor_tensor(
                         t_ap, t_ap, scB[:, TW * m:TW * m + TW], ALU.mult),
                     reads=[("dscr", m % 2), ("scB", m)], writes=[("dscr", m % 2)])
                s.op("vector",
                     lambda e, m=m, t_ap=t_ap, xc=xc: e.scalar_tensor_tensor(
                         xc[:, TW * m:TW * m + TW], t_ap, 0.5,
                         xc[:, TW * m:TW * m + TW], ALU.mult, ALU.add),
                     reads=[("dscr", m % 2), ("x", xbk, m)], writes=[("x", xbk, m)])
                if tail_stats:
                    if m >= 1:
                        shadow_j(m - 1)
                        ln_sq_j(m - 1, [("x", xbk, m - 1)], xc, eng="gpsimd")
                    if m >= 3:
                        ln_mm_j(m - 3, "xb", xb)
                if tail_io is not None:
                    g_ = tail_io
                    if io_next:
                        ln_sq_j(m, [("x", nbk, m)], xn, eng="vector")
                        if m >= 2:
                            ln_mm_j(m - 2, "xb", xb)
                    else:
                        s.dma("gpsimd" if m % 2 == 0 else "scalar_q",
                              lambda e, m=m, g_=g_, xc=xc: e.dma_start(
                                  out=d_o[g_, m, :, :],
                                  in_=xc[:, TW * m:TW * m + TW]),
                              reads=[("x", xbk, m)])
            proj("wr", "scA", scA, evac_r)
            s.op("scalar",
                 lambda e: e.activation(warm[0:1, 0:1], oneb[0:1, 0:1], AF.Ln),
                 reads=[("oneb",)], writes=[("warm",)])
            if tail_stats:
                shadow_j(NJ - 1)
                ln_sq_j(NJ - 1, [("x", xbk, NJ - 1)], xc)
                ln_mm_j(NJ - 3, "xb", xb)
            if io_next:
                for m in range(NJ - 2, NJ):
                    ln_mm_j(m, "xb", xb)

        for g in range(ngroups):
            XC[0] = xbufs[g % 2]
            XN[0] = xbufs[(g + 1) % 2]
            XB[0] = g % 2
            if g == 0:
                for h in range(2):
                    s.dma("sync",
                          lambda e, h=h, g=g: e.dma_start(
                              out=x_res[:, 2048 * h:2048 * h + 2048]
                              .rearrange("p (j w) -> p j w", j=4),
                              in_=d_x[g, 4 * h:4 * h + 4, :, :].rearrange("j p w -> p j w")),
                          writes=[("x", 0, jj2) for jj2 in range(4 * h, 4 * h + 4)])
                s.dma("sync",
                      lambda e: e.dma_start(out=d_wg_s, in_=sb["W_wg"][:]),
                      reads=[("W", "wg")])
                for j in range(NJ):
                    shadow_j(j)
                    ln_sq_j(j, [("x", 0, j)], x_res, eng="vector")
                    if j >= 2:
                        ln_mm_j(j - 2, "xb", xb)
                done1 = NJ - 2
            else:
                done1 = NJ

            ln_finish("xb", xb, done_mms=done1)
            ffn(c1, "c1", tail_stats=True)

            if g > 0:
                pb2 = (g - 1) % 2
                for h in range(2):
                    s.dma("gpsimd" if h == 0 else "scalar_q",
                          lambda e, h=h, g=g, pb2=pb2: e.dma_start(
                              out=d_o[g - 1, 4 * h:4 * h + 4, :, :]
                              .rearrange("j p w -> p j w"),
                              in_=xbufs[pb2][:, 2048 * h:2048 * h + 2048]
                              .rearrange("p (j w) -> p j w", j=4)),
                          reads=[("x", pb2, jj2) for jj2 in range(4 * h, 4 * h + 4)])
            if g + 1 < ngroups:
                nb2 = (g + 1) % 2
                for h in range(2):
                    s.dma("sync",
                          lambda e, h=h, g=g, nb2=nb2: e.dma_start(
                              out=xbufs[nb2][:, 2048 * h:2048 * h + 2048]
                              .rearrange("p (j w) -> p j w", j=4),
                              in_=d_x[g + 1, 4 * h:4 * h + 4, :, :]
                              .rearrange("j p w -> p j w")),
                          writes=[("x", nb2, jj2) for jj2 in range(4 * h, 4 * h + 4)])
            emit_rs2 = ln_finish("xb", xb, done_mms=NJ - 2, late_rs=True)
            for j in range(NJ):
                u_ap = scE[:, TW * j:TW * j + TW]
                s.op("vector",
                     lambda e, j=j, u_ap=u_ap: e.tensor_tensor(
                         u_ap, xb[:, TW * j:TW * j + TW], lnb[:, 0:512], ALU.add),
                     reads=[("xb", j), ("lnb_nm",)], writes=[("u", j), ("uu", j)])
                s.op("vector",
                     lambda e, j=j, u_ap=u_ap: e.tensor_tensor(
                         xb[:, TW * j:TW * j + TW], u_ap, lnb[:, 512:1024], ALU.mult),
                     reads=[("uu", j), ("lnb_rs",)], writes=[("xb", j)])

            def evac_q(m, pap, pkey):
                s.op("scalar", lambda e, m=m, pap=pap: e.copy(
                    scA[:, TW * m:TW * m + TW], pap),
                    reads=[(pkey,)], writes=[("scA", m)])
            proj("wq", "u", scE, evac_q, chase=True, mid=emit_rs2, mid_at=1)

            def evac_kk(m, pap, pkey):
                s.op("scalar", lambda e, m=m, pap=pap: e.copy(
                    scB[:, TW * m:TW * m + TW], pap),
                    reads=[(pkey,)], writes=[("scB", m)])
            proj("wk", "xb", xb, evac_kk)

            def evac_g(m, pap, pkey):
                j = m
                sg_ap = sgscr[:, 0:512]
                loga_ap = sgscr[:, 0:512]
                lam_ap = lams[:, 0:512]
                ep_ap = eps_[:, 0:512]
                em_ap = ems[:, 0:512]
                s.op("scalar", lambda e, pap=pap, sg_ap=sg_ap: e.activation(
                    sg_ap, pap, AF.Exp, scale=-1.0),
                    reads=[(pkey,)], writes=[("sgs", 0)])
                s.op("scalar",
                     lambda e, sg_ap=sg_ap, loga_ap=loga_ap: e.activation(
                         loga_ap, sg_ap, AF.Ln, bias=oneb[:]),
                     reads=[("sgs", 0), ("oneb",)], writes=[("loga", 0)])
                for cb in range(8):
                    s.op("vector",
                         lambda e, cb=cb, loga_ap=loga_ap, lam_ap=lam_ap:
                         e.tensor_tensor_scan(
                             lam_ap[:, CH * cb:CH * cb + CH],
                             loga_ap[:, CH * cb:CH * cb + CH],
                             zeros64[:], 0.0, ALU.add, ALU.add),
                         reads=[("loga", 0), ("zeros64",)], writes=[("lam", cb)])
                lam_keys = [("lam", cb) for cb in range(8)]
                s.op("scalar",
                     lambda e, lam_ap=lam_ap, ep_ap=ep_ap: e.activation(
                         ep_ap, lam_ap, AF.Exp, scale=-1.0),
                     reads=lam_keys, writes=[("ep", 0)])
                s.op("vector",
                     lambda e, ep_ap=ep_ap: e.tensor_tensor(
                         ep_ap, ep_ap, lnb[:, 512:1024], ALU.mult),
                     reads=[("ep", 0), ("lnb_rs",)], writes=[("ep", 0)])
                s.op("scalar",
                     lambda e, lam_ap=lam_ap, em_ap=em_ap: e.activation(
                         em_ap, lam_ap, AF.Exp),
                     reads=lam_keys, writes=[("em", 0)])

                def fac(e, j=j, lam_ap=lam_ap):
                    lsrc = lam_ap.rearrange("p (c b t) -> p c b t", c=2, b=4)[:, :, :, CH - 1]
                    dv = acs[:, 8 * j:8 * j + 8].rearrange("p (b c) -> p b c", b=4)
                    return e.activation(dv, lsrc.rearrange("p c b -> p b c"), AF.Exp,
                                        scale=-1.0)
                s.op("scalar", fac, reads=lam_keys,
                     writes=[("ac", j)])
                s.op("vector",
                     lambda e, j=j, ep_ap=ep_ap: e.tensor_tensor(
                         scA[:, TW * j:TW * j + TW], scA[:, TW * j:TW * j + TW],
                         ep_ap, ALU.mult),
                     reads=[("scA", j), ("ep", 0)], writes=[("scA", j)])
                s.op("vector",
                     lambda e, j=j, em_ap=em_ap: e.tensor_tensor(
                         scB[:, TW * j:TW * j + TW], scB[:, TW * j:TW * j + TW],
                         em_ap, ALU.mult),
                     reads=[("scB", j), ("em", 0)], writes=[("scB", j)])

            def g_mtile(m):
                pi = pcnt[0] % len(pp)
                pcnt[0] += 1
                P, Pk = pp[pi], ppk[pi]
                wt = sb["W_wg"]

                def fn(e, m=m, P=P, wt=wt):
                    for k in range(NJ):
                        inst = MM(e, P[:, 0:TW],
                                  wt[:, D * k + 128 * m: D * k + 128 * m + 128],
                                  xb[:, TW * k: TW * k + TW],
                                  start=(k == 0), stop=(k == NJ - 1))
                    return inst
                s.op("tensor", fn,
                     reads=[("W", "wg")] + [("xb", k) for k in range(NJ)],
                     writes=[(Pk,)])
                evac_g(m, P[:, 0:TW], Pk)

            def vt_fill(bp, c, n):
                pi = pcnt[0] % len(pp)
                pcnt[0] += 1
                P, Pk = pp[pi], ppk[pi]

                def fvt(e, bp=bp, c=c, n=n, P=P):
                    wt = sb["W_wv"]
                    for k in range(NJ):
                        inst = MM(e,
                            P[0:128, 0:512],
                            xb[:, TW * k + 256 * c + 128 * bp:
                                TW * k + 256 * c + 128 * bp + 128],
                            wt[:, D * k + 512 * n: D * k + 512 * n + 512],
                            start=(k == 0), stop=(k == NJ - 1))
                    return inst
                s.op("tensor", fvt,
                     reads=[("W", "wv")] + [("xb", k) for k in range(NJ)],
                     writes=[(Pk,)])
                s.op("vector",
                     lambda e, bp=bp, c=c, n=n, P=P: e.tensor_copy(
                         VT[:, 2048 * bp + 1024 * c + 512 * n:
                            2048 * bp + 1024 * c + 512 * n + 512],
                         P[:, 0:512]),
                     reads=[(Pk,)], writes=[("VTb", 4 * bp + 2 * c + n)])

            vt_order = [(bp, c, n) for bp in range(2) for c in range(2) for n in range(2)]
            for m in range(NJ):
                g_mtile(m)
                vt_fill(*vt_order[m])
            s.dma("scalar_q",
                  lambda e: e.dma_start(out=sb["W_wg"][:], in_=d_wo_s),
                  writes=[("W", "wg")])

            units = [(c, j, bp) for c in range(2) for j in range(NJ) for bp in range(2)]

            kslot = {}
            kcnt = [0]

            def emit_ktT(c, j):
                jj = kcnt[0] % 3
                kcnt[0] += 1
                kslot[(c, j)] = jj
                for bp in range(2):
                    s.dma("sync",
                          lambda e, c=c, j=j, bp=bp, jj=jj: e.dma_start(
                              out=ktT[:, 256 * jj + 128 * bp: 256 * jj + 128 * bp + 128],
                              in_=scB[:, TW * j + 256 * c + 128 * bp:
                                      TW * j + 256 * c + 128 * bp + 128],
                              transpose=True),
                          reads=[("scB", j)], writes=[("ktT", jj, bp)])

            PAlist = [PA[0], PA[1], PA[2], P0]
            PAkeys = ["PA0", "PA1", "PA2", "P0"]
            PYlist = [PY[0], PY[1], P1]
            PYkeys = ["PY0", "PY1", "P1"]

            def emit_front(n):
                c, j, bp = units[n]
                un = n % 4
                us_ = n % 3
                PAu = PAlist[un]
                PAk = PAkeys[un]
                st_ap = stS[:, 128 * us_:128 * us_ + 128]
                base = TW * j + 256 * c

                def fS(hh):
                    def _f(e, j=j, bp=bp, c=c, PAu=PAu, hh=hh, base=base):
                        for b01 in range(2):
                            b = 2 * bp + b01
                            inst = MM(e,
                                PAu[64 * b01:64 * b01 + 64, 64 * hh:64 * hh + 64],
                                scB[64 * hh:64 * hh + 64, base + CH * b: base + CH * b + CH],
                                scA[64 * hh:64 * hh + 64, base + CH * b: base + CH * b + CH],
                                start=(hh == 0), stop=(hh == 1),
                                tile_position=(64 * hh, 64 * b01)
                                if (hh or b01) else None)
                        return inst
                    return _f
                s.op("tensor", fS(0), reads=[("scB", j), ("scA", j)], writes=[(PAk,)])
                s.op("tensor", fS(1), reads=[("scB", j), ("scA", j)], writes=[(PAk,)])
                s.op("vector",
                     lambda e, st_ap=st_ap, PAu=PAu: e.tensor_tensor(
                         st_ap, PAu[:, 0:128], mask[:], ALU.mult),
                     reads=[(PAk,), ("mask",)], writes=[("st", us_)])

            def emit_back(n):
                c, j, bp = units[n]
                jj = kslot[(c, j)]
                un = n % 4
                us_ = n % 3
                uy = n % 3
                PAu, PYu = PAlist[un], PYlist[uy]
                PAk, PYk = PAkeys[un], PYkeys[uy]
                st_ap = stS[:, 128 * us_:128 * us_ + 128]
                base = TW * j + 256 * c
                vbase = 2048 * bp + 1024 * c

                def fY_intra(b01):
                    def _f(e, j=j, bp=bp, PYu=PYu, st_ap=st_ap, b01=b01, vbase=vbase):
                        for hh in range(2):
                            inst = MM(e,
                                PYu[64 * hh:64 * hh + 64, 64 * b01:64 * b01 + 64],
                                VT[64 * b01:64 * b01 + 64,
                                   vbase + 64 * (2 * j + hh): vbase + 64 * (2 * j + hh) + 64],
                                st_ap[64 * b01:64 * b01 + 64, 64 * hh:64 * hh + 64],
                                start=(b01 == 0), stop=False,
                                tile_position=(64 * b01, 64 * hh)
                                if (b01 or hh) else None)
                        return inst
                    return _f

                def fY_inter(hh):
                    def _f(e, j=j, bp=bp, PYu=PYu, hh=hh, base=base):
                        for b01 in range(2):
                            b = 2 * bp + b01
                            inst = MM(e,
                                PYu[64 * hh:64 * hh + 64, 64 * b01:64 * b01 + 64],
                                hst[64 * hh:64 * hh + 64,
                                    64 * (4 * j + b): 64 * (4 * j + b) + 64],
                                scA[64 * hh:64 * hh + 64, base + CH * b: base + CH * b + CH],
                                start=False, stop=(b01 == 1),
                                tile_position=(64 * hh, 64 * hh) if hh else None)
                        return inst
                    return _f
                yreads = [("st", us_), ("VTb", 4 * bp + 2 * c + j // 4), ("scA", j),
                          ("h", j, 2 * bp), ("h", j, 2 * bp + 1)]
                s.op("tensor", fY_intra(0), reads=yreads, writes=[(PYk,)])
                s.op("tensor", fY_intra(1), reads=yreads, writes=[(PYk,)])
                s.op("tensor", fY_inter(0), reads=yreads, writes=[(PYk,)])
                s.op("tensor", fY_inter(1), reads=yreads, writes=[(PYk,)])

                def fH(b01):
                    def _f(e, j=j, bp=bp, PAu=PAu, b01=b01, jj=jj, vbase=vbase):
                        for hh in range(2):
                            inst = MM(e,
                                PAu[64 * hh:64 * hh + 64,
                                    128 + 64 * b01: 128 + 64 * b01 + 64],
                                ktT[64 * b01:64 * b01 + 64,
                                    256 * jj + 128 * bp + 64 * hh:
                                    256 * jj + 128 * bp + 64 * hh + 64],
                                VT[64 * b01:64 * b01 + 64,
                                   vbase + 64 * (2 * j + hh): vbase + 64 * (2 * j + hh) + 64],
                                start=(b01 == 0), stop=(b01 == 1),
                                tile_position=(64 * b01, 64 * hh)
                                if (b01 or hh) else None)
                        return inst
                    return _f
                hreads = [("ktT", jj, bp), ("VTb", 4 * bp + 2 * c + j // 4), ("st", us_)]
                s.op("tensor", fH(0), reads=hreads, writes=[(PAk,)])
                s.op("tensor", fH(1), reads=hreads, writes=[(PAk,)])
                ht_ap = htmp[:, 128 * us_:128 * us_ + 128]
                s.op("vector",
                     lambda e, j=j, bp=bp, PAu=PAu, ht_ap=ht_ap: e.tensor_tensor(
                         ht_ap,
                         hst[:, 64 * (4 * j + 2 * bp): 64 * (4 * j + 2 * bp) + 128],
                         PAu[:, 128:256], ALU.add),
                     reads=[(PAk,), ("h", j, 2 * bp), ("h", j, 2 * bp + 1)],
                     writes=[("htmp", us_)])
                for b01 in range(2):
                    b = 2 * bp + b01
                    acc = 8 * j + 2 * b + c
                    s.op("vector",
                         lambda e, j=j, b=b, b01=b01, acc=acc, ht_ap=ht_ap:
                         e.tensor_single_scalar(
                             hst[:, 64 * (4 * j + b): 64 * (4 * j + b) + 64],
                             ht_ap[:, 64 * b01:64 * b01 + 64],
                             acs[:, acc:acc + 1], ALU.mult),
                         reads=[("htmp", us_), ("ac", j)], writes=[("h", j, b)])
                s.op("scalar",
                     lambda e, j=j, bp=bp, base=base, PYu=PYu: e.copy(
                         scE[:, base + 128 * bp: base + 128 * bp + 128],
                         PYu[:, 0:128]),
                     reads=[(PYk,)], writes=[("u", j)])

            emit_ktT(0, 0)
            emit_ktT(0, 1)
            emit_ktT(0, 2)
            emit_front(0)
            emit_front(1)
            emit_front(2)
            YMAP = [0, 4, 1, 5, 2, 6, 3, 7]

            def yln_sq_chase(nb):
                if nb >= 17 and (nb - 17) % 2 == 0:
                    jy = (nb - 17) // 2
                    if jy <= 5:
                        ln_sq_j(jy, [("u", jy)], scE, eng="gpsimd", blk=YMAP[jy])

            for n in range(3, len(units)):
                emit_back(n - 3)
                yln_sq_chase(n - 3)
                if n % 2 == 0:
                    cc, jn, _ = units[n]
                    jn2 = jn + 1
                    cn = cc
                    if jn2 >= NJ:
                        jn2 -= NJ
                        cn += 1
                    if cn < 2:
                        emit_ktT(cn, jn2)
                emit_front(n)
            for nb in (len(units) - 3, len(units) - 2, len(units) - 1):
                emit_back(nb)
                yln_sq_chase(nb)

            for jy in (6, 7):
                ln_sq_j(jy, [("u", jy)], scE, eng=("gpsimd" if jy % 2 else "vector"),
                        blk=YMAP[jy])
            for j in range(NJ):
                ln_mm_j(j, "u", scE, blk=YMAP[j])
            emit_rsY = ln_finish("u", scE, done_mms=NJ, late_rs=True)
            for j in range(NJ):
                ln_apply_y(j)

            xcw, xbw = XC[0], XB[0]

            def evac_o(m, pap, pkey):
                s.op("vector",
                     lambda e, m=m, pap=pap: e.tensor_tensor(
                         sgscr[:, 0:512], pap, lnb[:, 512:1024], ALU.mult),
                     reads=[(pkey,), ("lnb_rs",)],
                     writes=[("sgs", 0), ("stB",)])
                s.op("vector",
                     lambda e, m=m, xcw=xcw: e.scalar_tensor_tensor(
                         xcw[:, TW * m:TW * m + TW], sgscr[:, 0:512],
                         wob[:, m:m + 1],
                         xcw[:, TW * m:TW * m + TW], ALU.add, ALU.add),
                     reads=[("sgs", 0), ("x", xbw, m), ("wob",)],
                     writes=[("x", xbw, m)])
                if m >= 1:
                    shadow_j(m - 1)
                    ln_sq_j(m - 1, [("x", xbw, m - 1)], xcw, eng="gpsimd")
                if m >= 3:
                    ln_mm_j(m - 3, "xb", xb)
            proj("wo", "u", scE, evac_o, chase=True, mid=emit_rsY, mid_at=0,
                 mid_pre=True)
            if g + 1 < ngroups:
                s.dma("scalar_q",
                      lambda e: e.dma_start(out=sb["W_wg"][:], in_=d_wg_s),
                      writes=[("W", "wg")])
            shadow_j(NJ - 1)
            ln_sq_j(NJ - 1, [("x", xbw, NJ - 1)], xcw)
            ln_mm_j(NJ - 3, "xb", xb)

            ln_finish("xb", xb, done_mms=NJ - 2)
            ffn(c3, "c3", tail_stats=False, tail_io=g)

        s.emit(block, limit=limit)
    return nc


def make_host_consts():
    idf = np.eye(128, dtype=np.float32)
    m64 = np.triu(np.ones((CH, CH), dtype=np.float32))
    maskf = np.block([[m64, m64], [m64, m64]]).astype(np.float32)
    return idf, maskf


def host_pre(xc):
    v = xc.reshape(B_PER_CORE, NGROUPS_FULL, 2, CH, NJ, 128)
    v = v.transpose(1, 4, 5, 2, 0, 3)
    return np.ascontiguousarray(v.reshape(NGROUPS_FULL, NJ, 128, TPG))


def host_post(oc):
    v = oc.reshape(NGROUPS_FULL, NJ, 128, 2, B_PER_CORE, CH)
    v = v.transpose(4, 0, 3, 5, 1, 2)
    return np.ascontiguousarray(v.reshape(B_PER_CORE, T, D))


def make_in_maps(x, wT, vecs):
    idf, maskf = make_host_consts()
    in_maps = []
    for c in range(NCORES):
        m = {"x": host_pre(x[B_PER_CORE * c:B_PER_CORE * (c + 1)]),
             "vecs": vecs, "idf": idf, "maskf": maskf}
        for n in W_NAMES:
            m[n] = wT[n]
        in_maps.append(m)
    return in_maps


def prep_inputs(L0_attn_x, L0_attn_Wq, L0_attn_Wk, L0_attn_Wv, L0_attn_Wg, L0_attn_Wo,
                L0_attn_gamma, L0_attn_beta, L0_ffn_W_r, L0_ffn_W_k, L0_ffn_W_v,
                L0_ffn_mu_r, L0_ffn_mu_k):
    x = np.ascontiguousarray(np.asarray(L0_attn_x, dtype=np.float32))
    wmap = {
        "wr": L0_ffn_W_r, "wkf": L0_ffn_W_k, "wvf": L0_ffn_W_v,
        "wq": L0_attn_Wq, "wk": L0_attn_Wk, "wv": L0_attn_Wv,
        "wg": L0_attn_Wg, "wo": L0_attn_Wo,
    }
    wT = {n: np.ascontiguousarray(np.asarray(w, np.float32).T) for n, w in wmap.items()}
    vecs = np.zeros((128, 64), dtype=np.float32)
    vecs[:, 0:8] = np.asarray(L0_ffn_mu_r, np.float32).reshape(8, 128).T
    vecs[:, 8:16] = np.asarray(L0_ffn_mu_k, np.float32).reshape(8, 128).T
    vecs[:, 16:24] = np.asarray(L0_attn_gamma, np.float32).reshape(8, 128).T
    vecs[:, 24:32] = np.asarray(L0_attn_beta, np.float32).reshape(8, 128).T
    return x, wT, vecs


def kernel(**inputs):
    x, wT, vecs = prep_inputs(**inputs)
    nc = build_program(NGROUPS_FULL)
    in_maps = make_in_maps(x, wT, vecs)
    res = run_bass_kernel_spmd(nc, in_maps, core_ids=list(range(NCORES)))
    return np.concatenate([host_post(np.asarray(r["o"], np.float32))
                           for r in res.results], axis=0)





# version 77
# speedup vs baseline: 1.2080x; 1.0037x over previous
import numpy as np
import concourse.bass as bass
import concourse.mybir as mybir
from concourse.bass_utils import run_bass_kernel_spmd

F32 = mybir.dt.float32
BF16 = mybir.dt.bfloat16
AF = mybir.ActivationFunctionType
ALU = mybir.AluOpType

NCORES = 8
B_PER_CORE = 4
T = 2048
D = 1024
NJ = 8
CH = 64
TPG = 512
NGROUPS_FULL = T // 128

SAME_ENGINE_EDGES = True
NDMASLOT = 8


class Sched:
    def __init__(self, nc, es):
        self.nc = nc
        self.sems = {}
        self.counts = {}
        for e in ("tensor", "vector", "scalar", "gpsimd"):
            self.sems[e] = es.enter_context(nc.semaphore(f"sem_{e}"))
            self.counts[e] = 0
        for q in ("sync", "gpsimd", "scalar_q"):
            for i in range(NDMASLOT):
                k = ("q", q, i)
                self.sems[k] = es.enter_context(nc.semaphore(f"sem_{q}{i}"))
                self.counts[k] = 0
        self.dma_idx = {"sync": 0, "gpsimd": 0, "scalar_q": 0}
        self.ops = []
        self.writer = {}
        self.readers = {}

    def _record(self, eng, fn, reads, writes, sem_key, inc, extra_need=None):
        need = dict(extra_need or {})

        def add_dep(sk, cnt):
            if sk == sem_key and not SAME_ENGINE_EDGES:
                return
            if sk == sem_key and cnt >= self.counts[sem_key] + inc:
                return
            need[sk] = max(need.get(sk, 0), cnt)

        for k in reads:
            w = self.writer.get(k)
            if w:
                add_dep(*w)
        for k in writes:
            w = self.writer.get(k)
            if w:
                add_dep(*w)
            for r in self.readers.get(k, ()):
                add_dep(*r)
        after = self.counts[sem_key] + inc
        self.ops.append((eng, need, fn, sem_key, inc))
        self.counts[sem_key] = after
        for k in reads:
            self.readers.setdefault(k, []).append((sem_key, after))
        for k in writes:
            self.writer[k] = (sem_key, after)
            self.readers[k] = []
        return after

    def op(self, eng, fn, reads=(), writes=()):
        return self._record(eng, fn, reads, writes, eng, 1)

    def dma(self, q, fn, reads=(), writes=()):
        slot = self.dma_idx[q] % NDMASLOT
        self.dma_idx[q] += 1
        sk = ("q", q, slot)
        extra = {}
        if self.counts[sk] > 0:
            extra[sk] = self.counts[sk]
        return self._record(q, fn, reads, writes, sk, 16, extra)

    def emit(self, block, limit=None):
        all_keys = list(self.sems.keys())
        ops = self.ops if limit is None else self.ops[:limit]
        fincounts = {k: 0 for k in all_keys}
        for (e, need, fn, sem_key, inc) in ops:
            fincounts[sem_key] += inc

        qalias = {"scalar_q": "scalar"}

        def section(ename):
            def _(eng):
                hw = {k: 0 for k in all_keys}
                for (e, need, fn, sem_key, inc) in ops:
                    if qalias.get(e, e) != ename:
                        continue
                    for sk in all_keys:
                        cnt = need.get(sk)
                        if cnt is not None and cnt > hw[sk]:
                            eng.wait_ge(self.sems[sk], cnt)
                            hw[sk] = cnt
                    inst = fn(eng)
                    inst.then_inc(self.sems[sem_key], inc)
                if ename == "gpsimd":
                    for q in ("gpsimd", "sync", "scalar_q"):
                        for i in range(NDMASLOT):
                            k = ("q", q, i)
                            if fincounts[k] > 0:
                                eng.wait_ge(self.sems[k], fincounts[k])
            return _

        block.sync(section("sync"))
        block.vector(section("vector"))
        block.scalar(section("scalar"))
        block.tensor(section("tensor"))
        block.gpsimd(section("gpsimd"))


W_NAMES = ["wr", "wkf", "wvf", "wq", "wk", "wv", "wg", "wo"]


def MM(e, *a, **kw):
    kw.setdefault("skip_group_check", True)
    return e.matmul(*a, **kw)


def build_program(ngroups, limit=None):
    nc = bass.Bass("TRN2", target_bir_lowering=False, debug=False)
    d_x = nc.dram_tensor("x", [NGROUPS_FULL, NJ, 128, TPG], F32, kind="ExternalInput").ap()
    d_w = {n: nc.dram_tensor(n, [D, D], F32, kind="ExternalInput").ap() for n in W_NAMES}
    d_vecs = nc.dram_tensor("vecs", [128, 64], F32, kind="ExternalInput").ap()
    d_idf = nc.dram_tensor("idf", [128, 128], F32, kind="ExternalInput").ap()
    d_maskf = nc.dram_tensor("maskf", [128, 128], F32, kind="ExternalInput").ap()
    d_o = nc.dram_tensor("o", [NGROUPS_FULL, NJ, 128, TPG], F32, kind="ExternalOutput").ap()
    d_wg_s = nc.dram_tensor("wg_s", [128, NJ * D], BF16, kind="Internal").ap()
    d_wo_s = nc.dram_tensor("wo_s", [128, NJ * D], BF16, kind="Internal").ap()
    d_ws = nc.dram_tensor("ws", [NJ * 128], F32, kind="Internal").ap()

    import contextlib
    es = contextlib.ExitStack()
    with es:
        sb = {}

        def alloc(name, shape, dtype):
            sb[name] = es.enter_context(nc.sbuf_tensor("s_" + name, shape, dtype))
            return sb[name]

        for n in W_NAMES:
            if n == "wo":
                continue
            alloc("W_" + n, [128, NJ * D], BF16)
        sb["W_wo"] = sb["W_wg"]
        vecs = alloc("vecs", [128, 64], F32)
        idf = alloc("idf", [128, 128], F32)
        mask = alloc("mask", [128, 128], BF16)
        onesneg = alloc("onesneg", [128, 1], BF16)
        onespos = alloc("onespos", [128, 1], BF16)
        ones1f = alloc("ones1f", [1, 128], F32)
        ones1b = alloc("ones1b", [1, 128], BF16)
        zeros64 = alloc("zeros64", [128, 64], BF16)
        epsb = alloc("epsb", [1, 1], F32)
        oneb = alloc("oneb", [128, 1], F32)
        wob = alloc("wob", [128, 8], F32)
        betab = alloc("betab", [128, 8], BF16)
        x_res = alloc("x_res", [128, 4096], F32)
        x2 = alloc("x2", [128, 4096], F32)
        xb = alloc("xb", [128, 4096], BF16)
        scA = alloc("scA", [128, 4096], BF16)
        scB = alloc("scB", [128, 4096], BF16)
        scE = alloc("scE", [128, 4096], BF16)
        VT = alloc("VT", [128, 4096], BF16)
        hst = alloc("hst", [128, 2048], BF16)
        dscr = alloc("dscr", [128, 1024], BF16)
        sgscr = alloc("sgscr", [128, 512], F32)
        lams = alloc("lams", [128, 512], F32)
        eps_ = alloc("eps", [128, 512], BF16)
        ems = alloc("ems", [128, 512], BF16)
        ktT = alloc("ktT", [128, 768], BF16)
        stS = alloc("stS", [128, 384], BF16)
        htmp = alloc("htmp", [128, 384], BF16)
        acs = alloc("acs", [128, 64], F32)
        c1 = alloc("c1", [128, 32], BF16)
        c3 = alloc("c3", [128, 32], BF16)
        lnb = alloc("lnb", [128, 1024], BF16)
        warm = alloc("warm", [1, 4], F32)
        onesb1 = alloc("onesb1", [128, 1], BF16)
        wsumb = alloc("wsumb", [1, 1024], BF16)

        P0 = es.enter_context(nc.psum_tensor("P0", [128, 512], F32))
        P1 = es.enter_context(nc.psum_tensor("P1", [128, 512], F32))
        P2 = es.enter_context(nc.psum_tensor("P2", [128, 512], F32))
        PA = [es.enter_context(nc.psum_tensor(f"PA{i}", [128, 512], F32)) for i in range(3)]
        PY = [es.enter_context(nc.psum_tensor(f"PY{i}", [128, 512], F32)) for i in range(2)]
        pp = [P0, P1, PA[0], PA[1], PA[2], PY[0], PY[1]]
        ppk = ["P0", "P1", "PA0", "PA1", "PA2", "PY0", "PY1"]

        s = Sched(nc, es)
        block = es.enter_context(nc.Block())

        s.op("vector", lambda e: e.memset(onesneg[:], -1.0 / 1024), writes=[("onesneg",)])
        s.op("vector", lambda e: e.memset(onespos[:], 1.0 / 1024), writes=[("onespos",)])
        s.op("vector", lambda e: e.memset(ones1f[:], 1.0), writes=[("ones1f",)])
        s.op("vector", lambda e: e.memset(ones1b[:], 1.0), writes=[("ones1b",)])
        s.op("vector", lambda e: e.memset(onesb1[:], 1.0), writes=[("onesb1",)])
        s.op("vector", lambda e: e.memset(zeros64[:], 0.0), writes=[("zeros64",)])
        s.op("vector", lambda e: e.memset(epsb[:], 1e-5), writes=[("epsb",)])
        s.op("vector", lambda e: e.memset(oneb[:], 1.0), writes=[("oneb",)])
        s.op("vector", lambda e: e.memset(hst[:], 0.0),
             writes=[("h", j, b) for j in range(NJ) for b in range(4)])
        s.op("vector", lambda e: e.memset(c1[:], 0.0), writes=[("c1", j) for j in range(NJ)])
        s.op("vector", lambda e: e.memset(c3[:], 0.0), writes=[("c3", j) for j in range(NJ)])
        s.dma("sync", lambda e: e.dma_start(out=vecs[:, 0:32], in_=d_vecs[:, 0:32]),
              writes=[("vecs",)])
        s.dma("sync", lambda e: e.dma_start(out=idf[:], in_=d_idf[:]), writes=[("idf",)])
        s.dma("sync", lambda e: e.dma_start(out=x_res[:, 0:128], in_=d_maskf[:]),
              writes=[("x", 0, 0)])
        s.op("scalar", lambda e: e.copy(mask[:], x_res[:, 0:128]),
             reads=[("x", 0, 0)], writes=[("mask",)])
        s.op("scalar", lambda e: e.copy(betab[:], vecs[:, 24:32]),
             reads=[("vecs",)], writes=[("betab",)])
        s.op("vector",
             lambda e: e.tensor_scalar(vecs[:, 32:48], vecs[:, 0:16], 1.0, -1.0,
                                       ALU.mult, ALU.add),
             reads=[("vecs",)], writes=[("vecs",)])
        s.dma("gpsimd",
              lambda e: e.dma_start(
                  out=sb["W_wg"].rearrange("p (i e) -> p i e", i=NJ),
                  in_=d_w["wo"].rearrange("(i p) e -> p i e", p=128)),
              writes=[("W", "wg")])

        def fwob(e):
            for m in range(NJ):
                for k in range(NJ):
                    inst = MM(e, P2[0:128, 4 * m:4 * m + 1],
                              sb["W_wo"][:, D * k + 128 * m: D * k + 128 * m + 128],
                              betab[:, k:k + 1],
                              start=(k == 0), stop=(k == NJ - 1))
            return inst
        s.op("tensor", fwob, reads=[("W", "wg"), ("betab",)], writes=[("P2",)])
        s.op("scalar",
             lambda e: e.copy(wob[:],
                              P2[0:128, 0:32].rearrange("p (m c) -> p m c", c=4)[:, :, 0]),
             reads=[("P2",)], writes=[("wob",)])
        for k in range(NJ):
            s.op("vector",
                 lambda e, k=k: e.tensor_single_scalar(
                     sb["W_wo"][:, D * k:D * (k + 1)], sb["W_wo"][:, D * k:D * (k + 1)],
                     vecs[:, 16 + k:16 + k + 1], ALU.mult),
                 reads=[("W", "wg"), ("vecs",), ("wob",)], writes=[("W", "wg")])
        def fwsum(e):
            for m in range(NJ):
                for k in range(NJ):
                    inst = MM(e, P2[0:128, 40 + m:41 + m],
                              sb["W_wo"][:, D * k + 128 * m: D * k + 128 * m + 128],
                              onesb1[:],
                              start=(k == 0), stop=(k == NJ - 1))
            return inst
        s.op("tensor", fwsum, reads=[("W", "wg"), ("onesb1",)], writes=[("P2",)])
        s.op("scalar", lambda e: e.copy(lams[:, 40:48], P2[0:128, 40:48]),
             reads=[("P2",)], writes=[("ws_s",)])
        s.op("tensor",
             lambda e: e.transpose(PA[2][0:8, 0:128], lams[:, 40:48], idf[:]),
             reads=[("ws_s",), ("idf",)], writes=[("PA2",)])
        s.op("scalar", lambda e: e.copy(lams[0:8, 128:256], PA[2][0:8, 0:128]),
             reads=[("PA2",)], writes=[("ws_t",)])
        s.dma("scalar_q",
              lambda e: e.dma_start(out=d_ws.rearrange("(m d) -> m d", d=128),
                                    in_=lams[0:8, 128:256]),
              reads=[("ws_t",)], writes=[("ws_d",)])
        s.dma("gpsimd",
              lambda e: e.dma_start(out=wsumb[:],
                                    in_=d_ws.rearrange("(o c) -> o c", o=1)),
              reads=[("ws_d",)], writes=[("wsum",)])
        s.dma("scalar_q",
              lambda e: e.dma_start(out=d_wo_s, in_=sb["W_wo"][:]),
              reads=[("W", "wg")])
        for n in ["wkf", "wvf", "wr", "wq", "wk", "wv"]:
            dst, src = sb["W_" + n], d_w[n]
            s.dma("gpsimd",
                  lambda e, dst=dst, src=src: e.dma_start(
                      out=dst.rearrange("p (i e) -> p i e", i=NJ),
                      in_=src.rearrange("(i p) e -> p i e", p=128)),
                  writes=[("W", n)])
        s.dma("gpsimd",
              lambda e: e.dma_start(
                  out=sb["W_wg"].rearrange("p (i e) -> p i e", i=NJ),
                  in_=d_w["wg"].rearrange("(i p) e -> p i e", p=128)),
              writes=[("W", "wg")])

        xbufs = [x_res, x2]
        XC = [x_res]
        XN = [x2]
        XB = [0]

        mur_col = lambda j: vecs[:, j:j + 1]
        muk_col = lambda j: vecs[:, 8 + j:8 + j + 1]
        gam_col = lambda j: vecs[:, 16 + j:16 + j + 1]
        murm1_col = lambda j: vecs[:, 32 + j:32 + j + 1]
        mukm1_col = lambda j: vecs[:, 40 + j:40 + j + 1]

        pcnt = [0]
        TW = TPG

        WKEY = {"wo": "wg"}

        def wkey(wname):
            return ("W", WKEY.get(wname, wname))

        def proj(wname, src_name, src_t, evac, chase=False, mid=None, split=False,
                 mid_at=0, mid_pre=False, extra_k=None, extra_reads=()):
            wt = sb["W_" + wname]
            if split:
                nsplit = NJ - 1 if split == "nop2" else NJ
                banks = []
                for m in range(nsplit - 1):
                    pi = pcnt[0] % len(pp)
                    pcnt[0] += 1
                    banks.append((pp[pi], ppk[pi]))
                if split == "nop2":
                    pi = pcnt[0] % len(pp)
                    pcnt[0] += 1
                    banks.append((pp[pi], ppk[pi]))
                else:
                    banks.append((P2, "P2"))
                KH = NJ // 2

                def half(m, lo, P):
                    def fn(e, m=m, lo=lo, P=P):
                        for k in range(lo, lo + KH):
                            inst = MM(e, P[:, 0:TW],
                                      wt[:, D * k + 128 * m: D * k + 128 * m + 128],
                                      src_t[:, TW * k: TW * k + TW],
                                      start=(k == 0), stop=(k == NJ - 1))
                        return inst
                    return fn
                P0_, Pk0 = banks[0]
                for k in range(KH):
                    s.op("tensor",
                         lambda e, k=k, P=P0_: MM(
                             e, P[:, 0:TW],
                             wt[:, D * k: D * k + 128],
                             src_t[:, TW * k: TW * k + TW],
                             start=(k == 0), stop=False),
                         reads=[wkey(wname), (src_name, k)], writes=[(Pk0,)])
                for m in range(1, nsplit):
                    P, Pk = banks[m]
                    s.op("tensor", half(m, 0, P),
                         reads=[wkey(wname)] + [(src_name, k) for k in range(KH)],
                         writes=[(Pk,)])
                for k in range(KH, NJ):
                    s.op("tensor",
                         lambda e, k=k, P=P0_: MM(
                             e, P[:, 0:TW],
                             wt[:, D * k: D * k + 128],
                             src_t[:, TW * k: TW * k + TW],
                             start=False, stop=(k == NJ - 1)),
                         reads=[wkey(wname), (src_name, k)], writes=[(Pk0,)])
                evac(0, P0_[:, 0:TW], Pk0)
                for m in range(1, nsplit):
                    P, Pk = banks[m]
                    s.op("tensor", half(m, KH, P),
                         reads=[wkey(wname)] + [(src_name, k) for k in range(KH, NJ)],
                         writes=[(Pk,)])
                    evac(m, P[:, 0:TW], Pk)
                for m in range(nsplit, NJ):
                    P, Pk = banks[m - 1]

                    def full(e, m=m, P=P):
                        for k in range(NJ):
                            inst = MM(e, P[:, 0:TW],
                                      wt[:, D * k + 128 * m: D * k + 128 * m + 128],
                                      src_t[:, TW * k: TW * k + TW],
                                      start=(k == 0), stop=(k == NJ - 1))
                        return inst
                    s.op("tensor", full,
                         reads=[wkey(wname)] + [(src_name, k) for k in range(NJ)],
                         writes=[(Pk,)])
                    evac(m, P[:, 0:TW], Pk)
                return
            for m in range(NJ):
                pi = pcnt[0] % len(pp)
                pcnt[0] += 1
                P, Pk = pp[pi], ppk[pi]
                if chase and m == 0:
                    for k in range(NJ):
                        s.op("tensor",
                             lambda e, k=k, P=P, wt=wt, src_t=src_t: MM(
                                 e, P[:, 0:TW],
                                 wt[:, D * k: D * k + 128],
                                 src_t[:, TW * k: TW * k + TW],
                                 start=(k == 0),
                                 stop=(k == NJ - 1 and extra_k is None)),
                             reads=[wkey(wname), (src_name, k)], writes=[(Pk,)])
                    if extra_k is not None:
                        s.op("tensor",
                             lambda e, P=P: extra_k(e, P, 0),
                             reads=list(extra_reads), writes=[(Pk,)])
                else:
                    def fn(e, m=m, P=P, wt=wt, src_t=src_t):
                        for k in range(NJ):
                            inst = MM(e, P[:, 0:TW],
                                      wt[:, D * k + 128 * m: D * k + 128 * m + 128],
                                      src_t[:, TW * k: TW * k + TW],
                                      start=(k == 0),
                                      stop=(k == NJ - 1 and extra_k is None))
                        if extra_k is not None:
                            inst = extra_k(e, P, m)
                        return inst
                    s.op("tensor", fn,
                         reads=[wkey(wname)] + [(src_name, k) for k in range(NJ)]
                         + list(extra_reads),
                         writes=[(Pk,)])
                if m == mid_at and mid is not None and mid_pre:
                    mid()
                evac(m, P[:, 0:TW], Pk)
                if m == mid_at and mid is not None and not mid_pre:
                    mid()

        def ln_sq_j(j, src_keys, src_t, eng="gpsimd", blk=None):
            b = j if blk is None else blk
            sq_ap = VT[:, TW * b: TW * b + TW]
            s.op(eng if eng in ("gpsimd", "vector") else "vector",
                 lambda e, j=j, sq_ap=sq_ap, src_t=src_t: e.tensor_tensor(
                     sq_ap, src_t[:, TW * j:TW * j + TW],
                     src_t[:, TW * j:TW * j + TW], ALU.mult),
                 reads=src_keys, writes=[("VTb", b)])

        def ln_mm_j(j, src_name, src_t, blk=None):
            b = j if blk is None else blk
            def fnmm(e, j=j, b=b, src_t=src_t):
                MM(e, P2[0:1, 0:TW], onesneg[:],
                   src_t[:, TW * j:TW * j + TW],
                   start=(j == 0), stop=False)
                return MM(e, P2[64:65, 0:TW], onespos[:],
                          VT[:, TW * b:TW * b + TW],
                          start=(j == 0), stop=(j == NJ - 1),
                          tile_position=(0, 64))
            s.op("tensor", fnmm,
                 reads=[(src_name, j), ("VTb", b), ("onesneg",), ("onespos",)],
                 writes=[("P2",)])

        def ln_finish(src_name, src_t, done_mms=0, late_rs=False):
            for j in range(done_mms, NJ):
                ln_mm_j(j, src_name, src_t)
            s.op("scalar", lambda e: e.copy(ems[0:1, 0:512], P2[0:1, 0:512]),
                 reads=[("P2",)], writes=[("em", 0)])

            def fb1(e):
                return MM(e, PA[2][:, 0:512], ones1b[:], ems[0:1, 0:512],
                          start=True, stop=True)
            s.op("tensor", fb1, reads=[("em", 0), ("ones1b",)], writes=[("PA2",)])
            s.op("scalar", lambda e: e.copy(lnb[:, 0:512], PA[2][:, 0:512]),
                 reads=[("PA2",)], writes=[("lnb_nm",)])
            s.op("scalar", lambda e: e.activation(sgscr[0:1, 0:512], P2[0:1, 0:512],
                                                  AF.Square),
                 reads=[("P2",)], writes=[("stB",)])
            s.op("vector", lambda e: e.tensor_tensor(sgscr[0:1, 0:512], P2[64:65, 0:512],
                                                     sgscr[0:1, 0:512], ALU.subtract),
                 reads=[("P2",), ("stB",)], writes=[("stB",)])
            s.op("scalar", lambda e: e.activation(sgscr[0:1, 0:512], sgscr[0:1, 0:512],
                                                  AF.Ln, bias=epsb[:]),
                 reads=[("stB",), ("epsb",)], writes=[("stB",)])
            s.op("scalar", lambda e: e.activation(eps_[0:1, 0:512], sgscr[0:1, 0:512],
                                                  AF.Exp, scale=-0.5),
                 reads=[("stB",)], writes=[("ep", 0)])

            def emit_rs():
                def fb2(e):
                    return MM(e, P2[:, 0:512], ones1b[:], eps_[0:1, 0:512],
                              start=True, stop=True)
                s.op("tensor", fb2, reads=[("ep", 0), ("ones1b",), ("P2",)],
                     writes=[("P2",)])
                s.op("scalar", lambda e: e.copy(lnb[:, 512:1024], P2[:, 0:512]),
                     reads=[("P2",)], writes=[("lnb_rs",)])
            if late_rs:
                return emit_rs
            emit_rs()

        def ln_apply_x(j):
            u_ap = scE[:, TW * j:TW * j + TW]
            s.op("gpsimd",
                 lambda e, j=j, u_ap=u_ap: e.tensor_tensor(
                     u_ap, xb[:, TW * j:TW * j + TW], lnb[:, 0:512], ALU.add),
                 reads=[("xb", j), ("lnb_nm",)], writes=[("u", j), ("uu", j)])
            s.op("vector",
                 lambda e, j=j, u_ap=u_ap: e.tensor_tensor(
                     u_ap, u_ap, lnb[:, 512:1024], ALU.mult),
                 reads=[("u", j), ("lnb_rs",)], writes=[("u", j)])

        def ln_apply_y(j):
            t_ap = scE[:, TW * j:TW * j + TW]
            s.op("vector",
                 lambda e, j=j, t_ap=t_ap: e.tensor_tensor(
                     t_ap, t_ap, lnb[:, 0:512], ALU.add),
                 reads=[("u", j), ("lnb_nm",)], writes=[("u", j)])

        def shadow_j(j, src_t=None, bkey=None):
            st = XC[0] if src_t is None else src_t
            bk = XB[0] if bkey is None else bkey
            s.op("scalar",
                 lambda e, j=j, st=st: e.copy(xb[:, TW * j:TW * j + TW],
                                              st[:, TW * j:TW * j + TW]),
                 reads=[("x", bk, j)], writes=[("xb", j)])

        def ffn(carry_t, carry_key, tail_stats, tail_io=None):
            xc, xn, xbk = XC[0], XN[0], XB[0]
            nbk = 1 - xbk
            io_next = tail_io is not None and tail_io + 1 < ngroups

            def d_shift(j, u_ap, d_ap):
                s.op("vector",
                     lambda e, d_ap=d_ap, u_ap=u_ap: e.tensor_tensor(
                         d_ap[:, 1:512], u_ap[:, 1:512], u_ap[:, 0:511], ALU.subtract),
                     reads=[("uu", j)], writes=[("dscr", j % 2)])
                def fx0(e, j=j, d_ap=d_ap, u_ap=u_ap):
                    dv = d_ap.rearrange("p (x t) -> p x t", t=CH)[:, 0:4, 0]
                    uv = u_ap.rearrange("p (x t) -> p x t", t=CH)[:, 0:4, 0]
                    return e.tensor_tensor(dv, uv, carry_t[:, 4 * j:4 * j + 4],
                                           ALU.subtract)
                s.op("vector", fx0, reads=[("uu", j), (carry_key, j)],
                     writes=[("dscr", j % 2)])
                def fx1(e, j=j, d_ap=d_ap, u_ap=u_ap):
                    dv = d_ap.rearrange("p (x t) -> p x t", t=CH)[:, 4:8, 0]
                    uv = u_ap.rearrange("p (x t) -> p x t", t=CH)[:, 4:8, 0]
                    up = u_ap.rearrange("p (x t) -> p x t", t=CH)[:, 0:4, CH - 1]
                    return e.tensor_tensor(dv, uv, up, ALU.subtract)
                s.op("vector", fx1, reads=[("uu", j)], writes=[("dscr", j % 2)])

            for j in range(NJ):
                d_ap = dscr[:, 512 * (j % 2): 512 * (j % 2) + 512]
                u_ap = scE[:, TW * j:TW * j + TW]
                s.op("gpsimd",
                     lambda e, j=j, u_ap=u_ap: e.tensor_tensor(
                         u_ap, xb[:, TW * j:TW * j + TW], lnb[:, 0:512], ALU.add),
                     reads=[("xb", j), ("lnb_nm",)], writes=[("uu", j), ("u", j)])
                s.op("vector",
                     lambda e, j=j, u_ap=u_ap: e.tensor_tensor(
                         u_ap, u_ap, lnb[:, 512:1024], ALU.mult),
                     reads=[("uu", j), ("lnb_rs",)], writes=[("uu", j)])
                d_shift(j, u_ap, d_ap)
                s.op("vector",
                     lambda e, j=j, d_ap=d_ap, u_ap=u_ap: e.scalar_tensor_tensor(
                         scB[:, TW * j:TW * j + TW], d_ap, mukm1_col(j), u_ap,
                         ALU.mult, ALU.add),
                     reads=[("dscr", j % 2), ("uu", j), ("vecs",)], writes=[("scB", j)])

            def evac_k(m, pap, pkey):
                kr_ap = dscr[:, 512 * (m % 2): 512 * (m % 2) + 512]
                s.op("scalar",
                     lambda e, pap=pap, kr_ap=kr_ap: e.activation(kr_ap, pap, AF.Relu),
                     reads=[(pkey,)], writes=[("dscr", m % 2)])
                s.op("vector",
                     lambda e, m=m, kr_ap=kr_ap: e.tensor_tensor(
                         VT[:, TW * m:TW * m + TW], kr_ap, kr_ap, ALU.mult),
                     reads=[("dscr", m % 2)], writes=[("VTb", m)])
                if io_next:
                    shadow_j(m, src_t=xn, bkey=nbk)
            proj("wkf", "scB", scB, evac_k, split=True)

            for j in range(NJ):
                d_ap = dscr[:, 512 * (j % 2): 512 * (j % 2) + 512]
                u_ap = scE[:, TW * j:TW * j + TW]
                d_shift(j, u_ap, d_ap)
                s.op("vector",
                     lambda e, j=j, d_ap=d_ap, u_ap=u_ap: e.scalar_tensor_tensor(
                         scA[:, TW * j:TW * j + TW], d_ap, murm1_col(j), u_ap,
                         ALU.mult, ALU.add),
                     reads=[("dscr", j % 2), ("uu", j), ("vecs",)], writes=[("scA", j)])

            def evac_v(m, pap, pkey):
                s.op("scalar",
                     lambda e, m=m, pap=pap: e.copy(
                         scB[:, TW * m:TW * m + TW], pap),
                     reads=[(pkey,)], writes=[("scB", m)])
                if m == 5:
                    s.op("scalar",
                         lambda e: e.activation(warm[0:1, 0:1], oneb[0:1, 0:1],
                                                AF.Sigmoid),
                         reads=[("oneb",)], writes=[("warm",)])
            proj("wvf", "VTb", VT, evac_v, chase=True)
            for j in range(NJ):
                def fc(e, j=j):
                    uv = scE[:, TW * j:TW * j + TW].rearrange(
                        "p (x t) -> p x t", t=CH)[:, 4:8, CH - 1]
                    return e.copy(carry_t[:, 4 * j:4 * j + 4], uv)
                s.op("scalar", fc, reads=[("uu", j)], writes=[(carry_key, j)])

            def evac_r(m, pap, pkey):
                t_ap = dscr[:, 512 * (m % 2): 512 * (m % 2) + 512]
                s.op("scalar",
                     lambda e, pap=pap, t_ap=t_ap: e.activation(t_ap, pap, AF.Sigmoid),
                     reads=[(pkey,)], writes=[("dscr", m % 2)])
                s.op("vector",
                     lambda e, m=m, t_ap=t_ap: e.tensor_tensor(
                         t_ap, t_ap, scB[:, TW * m:TW * m + TW], ALU.mult),
                     reads=[("dscr", m % 2), ("scB", m)], writes=[("dscr", m % 2)])
                s.op("vector",
                     lambda e, m=m, t_ap=t_ap, xc=xc: e.scalar_tensor_tensor(
                         xc[:, TW * m:TW * m + TW], t_ap, 0.5,
                         xc[:, TW * m:TW * m + TW], ALU.mult, ALU.add),
                     reads=[("dscr", m % 2), ("x", xbk, m)], writes=[("x", xbk, m)])
                if tail_stats:
                    if m >= 1:
                        shadow_j(m - 1)
                        ln_sq_j(m - 1, [("x", xbk, m - 1)], xc, eng="gpsimd")
                    if m >= 3:
                        ln_mm_j(m - 3, "xb", xb)
                if tail_io is not None:
                    g_ = tail_io
                    if io_next:
                        ln_sq_j(m, [("x", nbk, m)], xn, eng="vector")
                        if m >= 2:
                            ln_mm_j(m - 2, "xb", xb)
                    else:
                        s.dma("gpsimd" if m % 2 == 0 else "scalar_q",
                              lambda e, m=m, g_=g_, xc=xc: e.dma_start(
                                  out=d_o[g_, m, :, :],
                                  in_=xc[:, TW * m:TW * m + TW]),
                              reads=[("x", xbk, m)])
            proj("wr", "scA", scA, evac_r)
            s.op("scalar",
                 lambda e: e.activation(warm[0:1, 0:1], oneb[0:1, 0:1], AF.Ln),
                 reads=[("oneb",)], writes=[("warm",)])
            if tail_stats:
                shadow_j(NJ - 1)
                ln_sq_j(NJ - 1, [("x", xbk, NJ - 1)], xc)
                ln_mm_j(NJ - 3, "xb", xb)
            if io_next:
                for m in range(NJ - 2, NJ):
                    ln_mm_j(m, "xb", xb)

        for g in range(ngroups):
            XC[0] = xbufs[g % 2]
            XN[0] = xbufs[(g + 1) % 2]
            XB[0] = g % 2
            if g == 0:
                for h in range(2):
                    s.dma("sync",
                          lambda e, h=h, g=g: e.dma_start(
                              out=x_res[:, 2048 * h:2048 * h + 2048]
                              .rearrange("p (j w) -> p j w", j=4),
                              in_=d_x[g, 4 * h:4 * h + 4, :, :].rearrange("j p w -> p j w")),
                          writes=[("x", 0, jj2) for jj2 in range(4 * h, 4 * h + 4)])
                s.dma("sync",
                      lambda e: e.dma_start(out=d_wg_s, in_=sb["W_wg"][:]),
                      reads=[("W", "wg")])
                for j in range(NJ):
                    shadow_j(j)
                    ln_sq_j(j, [("x", 0, j)], x_res, eng="vector")
                    if j >= 2:
                        ln_mm_j(j - 2, "xb", xb)
                done1 = NJ - 2
            else:
                done1 = NJ

            ln_finish("xb", xb, done_mms=done1)
            ffn(c1, "c1", tail_stats=True)

            if g > 0:
                pb2 = (g - 1) % 2
                for h in range(2):
                    s.dma("gpsimd" if h == 0 else "scalar_q",
                          lambda e, h=h, g=g, pb2=pb2: e.dma_start(
                              out=d_o[g - 1, 4 * h:4 * h + 4, :, :]
                              .rearrange("j p w -> p j w"),
                              in_=xbufs[pb2][:, 2048 * h:2048 * h + 2048]
                              .rearrange("p (j w) -> p j w", j=4)),
                          reads=[("x", pb2, jj2) for jj2 in range(4 * h, 4 * h + 4)])
            if g + 1 < ngroups:
                nb2 = (g + 1) % 2
                for h in range(2):
                    s.dma("sync",
                          lambda e, h=h, g=g, nb2=nb2: e.dma_start(
                              out=xbufs[nb2][:, 2048 * h:2048 * h + 2048]
                              .rearrange("p (j w) -> p j w", j=4),
                              in_=d_x[g + 1, 4 * h:4 * h + 4, :, :]
                              .rearrange("j p w -> p j w")),
                          writes=[("x", nb2, jj2) for jj2 in range(4 * h, 4 * h + 4)])
            emit_rs2 = ln_finish("xb", xb, done_mms=NJ - 2, late_rs=True)
            for j in range(NJ):
                u_ap = scE[:, TW * j:TW * j + TW]
                s.op("vector",
                     lambda e, j=j, u_ap=u_ap: e.tensor_tensor(
                         u_ap, xb[:, TW * j:TW * j + TW], lnb[:, 0:512], ALU.add),
                     reads=[("xb", j), ("lnb_nm",)], writes=[("u", j), ("uu", j)])
                s.op("vector",
                     lambda e, j=j, u_ap=u_ap: e.tensor_tensor(
                         xb[:, TW * j:TW * j + TW], u_ap, lnb[:, 512:1024], ALU.mult),
                     reads=[("uu", j), ("lnb_rs",)], writes=[("xb", j)])

            def evac_q(m, pap, pkey):
                s.op("scalar", lambda e, m=m, pap=pap: e.copy(
                    scA[:, TW * m:TW * m + TW], pap),
                    reads=[(pkey,)], writes=[("scA", m)])
            proj("wq", "u", scE, evac_q, chase=True, mid=emit_rs2, mid_at=1)

            def evac_kk(m, pap, pkey):
                s.op("scalar", lambda e, m=m, pap=pap: e.copy(
                    scB[:, TW * m:TW * m + TW], pap),
                    reads=[(pkey,)], writes=[("scB", m)])
            proj("wk", "xb", xb, evac_kk)

            def evac_g(m, pap, pkey):
                j = m
                sg_ap = sgscr[:, 0:512]
                loga_ap = sgscr[:, 0:512]
                lam_ap = lams[:, 0:512]
                ep_ap = eps_[:, 0:512]
                em_ap = ems[:, 0:512]
                s.op("scalar", lambda e, pap=pap, sg_ap=sg_ap: e.activation(
                    sg_ap, pap, AF.Exp, scale=-1.0),
                    reads=[(pkey,)], writes=[("sgs", 0)])
                s.op("scalar",
                     lambda e, sg_ap=sg_ap, loga_ap=loga_ap: e.activation(
                         loga_ap, sg_ap, AF.Ln, bias=oneb[:]),
                     reads=[("sgs", 0), ("oneb",)], writes=[("loga", 0)])
                for cb in range(8):
                    s.op("vector",
                         lambda e, cb=cb, loga_ap=loga_ap, lam_ap=lam_ap:
                         e.tensor_tensor_scan(
                             lam_ap[:, CH * cb:CH * cb + CH],
                             loga_ap[:, CH * cb:CH * cb + CH],
                             zeros64[:], 0.0, ALU.add, ALU.add),
                         reads=[("loga", 0), ("zeros64",)], writes=[("lam", cb)])
                lam_keys = [("lam", cb) for cb in range(8)]
                s.op("scalar",
                     lambda e, lam_ap=lam_ap, ep_ap=ep_ap: e.activation(
                         ep_ap, lam_ap, AF.Exp, scale=-1.0),
                     reads=lam_keys, writes=[("ep", 0)])
                s.op("vector",
                     lambda e, ep_ap=ep_ap: e.tensor_tensor(
                         ep_ap, ep_ap, lnb[:, 512:1024], ALU.mult),
                     reads=[("ep", 0), ("lnb_rs",)], writes=[("ep", 0)])
                s.op("scalar",
                     lambda e, lam_ap=lam_ap, em_ap=em_ap: e.activation(
                         em_ap, lam_ap, AF.Exp),
                     reads=lam_keys, writes=[("em", 0)])

                def fac(e, j=j, lam_ap=lam_ap):
                    lsrc = lam_ap.rearrange("p (c b t) -> p c b t", c=2, b=4)[:, :, :, CH - 1]
                    dv = acs[:, 8 * j:8 * j + 8].rearrange("p (b c) -> p b c", b=4)
                    return e.activation(dv, lsrc.rearrange("p c b -> p b c"), AF.Exp,
                                        scale=-1.0)
                s.op("scalar", fac, reads=lam_keys,
                     writes=[("ac", j)])
                s.op("vector",
                     lambda e, j=j, ep_ap=ep_ap: e.tensor_tensor(
                         scA[:, TW * j:TW * j + TW], scA[:, TW * j:TW * j + TW],
                         ep_ap, ALU.mult),
                     reads=[("scA", j), ("ep", 0)], writes=[("scA", j)])
                s.op("vector",
                     lambda e, j=j, em_ap=em_ap: e.tensor_tensor(
                         scB[:, TW * j:TW * j + TW], scB[:, TW * j:TW * j + TW],
                         em_ap, ALU.mult),
                     reads=[("scB", j), ("em", 0)], writes=[("scB", j)])

            def g_mtile(m):
                pi = pcnt[0] % len(pp)
                pcnt[0] += 1
                P, Pk = pp[pi], ppk[pi]
                wt = sb["W_wg"]

                def fn(e, m=m, P=P, wt=wt):
                    for k in range(NJ):
                        inst = MM(e, P[:, 0:TW],
                                  wt[:, D * k + 128 * m: D * k + 128 * m + 128],
                                  xb[:, TW * k: TW * k + TW],
                                  start=(k == 0), stop=(k == NJ - 1))
                    return inst
                s.op("tensor", fn,
                     reads=[("W", "wg")] + [("xb", k) for k in range(NJ)],
                     writes=[(Pk,)])
                evac_g(m, P[:, 0:TW], Pk)

            def vt_fill(bp, c, n):
                pi = pcnt[0] % len(pp)
                pcnt[0] += 1
                P, Pk = pp[pi], ppk[pi]

                def fvt(e, bp=bp, c=c, n=n, P=P):
                    wt = sb["W_wv"]
                    for k in range(NJ):
                        inst = MM(e,
                            P[0:128, 0:512],
                            xb[:, TW * k + 256 * c + 128 * bp:
                                TW * k + 256 * c + 128 * bp + 128],
                            wt[:, D * k + 512 * n: D * k + 512 * n + 512],
                            start=(k == 0), stop=(k == NJ - 1))
                    return inst
                s.op("tensor", fvt,
                     reads=[("W", "wv")] + [("xb", k) for k in range(NJ)],
                     writes=[(Pk,)])
                s.op("vector",
                     lambda e, bp=bp, c=c, n=n, P=P: e.tensor_copy(
                         VT[:, 2048 * bp + 1024 * c + 512 * n:
                            2048 * bp + 1024 * c + 512 * n + 512],
                         P[:, 0:512]),
                     reads=[(Pk,)], writes=[("VTb", 4 * bp + 2 * c + n)])

            vt_order = [(bp, c, n) for bp in range(2) for c in range(2) for n in range(2)]
            for m in range(NJ):
                g_mtile(m)
                vt_fill(*vt_order[m])
            s.dma("scalar_q",
                  lambda e: e.dma_start(out=sb["W_wg"][:], in_=d_wo_s),
                  writes=[("W", "wg")])

            units = [(c, j, bp) for c in range(2) for j in range(NJ) for bp in range(2)]

            kslot = {}
            kcnt = [0]

            def emit_ktT(c, j):
                jj = kcnt[0] % 3
                kcnt[0] += 1
                kslot[(c, j)] = jj
                for bp in range(2):
                    s.dma("sync",
                          lambda e, c=c, j=j, bp=bp, jj=jj: e.dma_start(
                              out=ktT[:, 256 * jj + 128 * bp: 256 * jj + 128 * bp + 128],
                              in_=scB[:, TW * j + 256 * c + 128 * bp:
                                      TW * j + 256 * c + 128 * bp + 128],
                              transpose=True),
                          reads=[("scB", j)], writes=[("ktT", jj, bp)])

            PAlist = [PA[0], PA[1], PA[2], P0]
            PAkeys = ["PA0", "PA1", "PA2", "P0"]
            PYlist = [PY[0], PY[1], P1]
            PYkeys = ["PY0", "PY1", "P1"]

            def emit_front(n):
                c, j, bp = units[n]
                un = n % 4
                us_ = n % 3
                PAu = PAlist[un]
                PAk = PAkeys[un]
                st_ap = stS[:, 128 * us_:128 * us_ + 128]
                base = TW * j + 256 * c

                def fS(hh):
                    def _f(e, j=j, bp=bp, c=c, PAu=PAu, hh=hh, base=base):
                        for b01 in range(2):
                            b = 2 * bp + b01
                            inst = MM(e,
                                PAu[64 * b01:64 * b01 + 64, 64 * hh:64 * hh + 64],
                                scB[64 * hh:64 * hh + 64, base + CH * b: base + CH * b + CH],
                                scA[64 * hh:64 * hh + 64, base + CH * b: base + CH * b + CH],
                                start=(hh == 0), stop=(hh == 1),
                                tile_position=(64 * hh, 64 * b01)
                                if (hh or b01) else None)
                        return inst
                    return _f
                s.op("tensor", fS(0), reads=[("scB", j), ("scA", j)], writes=[(PAk,)])
                s.op("tensor", fS(1), reads=[("scB", j), ("scA", j)], writes=[(PAk,)])
                s.op("vector",
                     lambda e, st_ap=st_ap, PAu=PAu: e.tensor_tensor(
                         st_ap, PAu[:, 0:128], mask[:], ALU.mult),
                     reads=[(PAk,), ("mask",)], writes=[("st", us_)])

            def emit_back(n):
                c, j, bp = units[n]
                jj = kslot[(c, j)]
                un = n % 4
                us_ = n % 3
                uy = n % 3
                PAu, PYu = PAlist[un], PYlist[uy]
                PAk, PYk = PAkeys[un], PYkeys[uy]
                st_ap = stS[:, 128 * us_:128 * us_ + 128]
                base = TW * j + 256 * c
                vbase = 2048 * bp + 1024 * c

                def fY_intra(b01):
                    def _f(e, j=j, bp=bp, PYu=PYu, st_ap=st_ap, b01=b01, vbase=vbase):
                        for hh in range(2):
                            inst = MM(e,
                                PYu[64 * hh:64 * hh + 64, 64 * b01:64 * b01 + 64],
                                VT[64 * b01:64 * b01 + 64,
                                   vbase + 64 * (2 * j + hh): vbase + 64 * (2 * j + hh) + 64],
                                st_ap[64 * b01:64 * b01 + 64, 64 * hh:64 * hh + 64],
                                start=(b01 == 0), stop=False,
                                tile_position=(64 * b01, 64 * hh)
                                if (b01 or hh) else None)
                        return inst
                    return _f

                def fY_inter(hh):
                    def _f(e, j=j, bp=bp, PYu=PYu, hh=hh, base=base):
                        for b01 in range(2):
                            b = 2 * bp + b01
                            inst = MM(e,
                                PYu[64 * hh:64 * hh + 64, 64 * b01:64 * b01 + 64],
                                hst[64 * hh:64 * hh + 64,
                                    64 * (4 * j + b): 64 * (4 * j + b) + 64],
                                scA[64 * hh:64 * hh + 64, base + CH * b: base + CH * b + CH],
                                start=False, stop=(b01 == 1),
                                tile_position=(64 * hh, 64 * hh) if hh else None)
                        return inst
                    return _f
                yreads = [("st", us_), ("VTb", 4 * bp + 2 * c + j // 4), ("scA", j),
                          ("h", j, 2 * bp), ("h", j, 2 * bp + 1)]
                s.op("tensor", fY_intra(0), reads=yreads, writes=[(PYk,)])
                s.op("tensor", fY_intra(1), reads=yreads, writes=[(PYk,)])
                s.op("tensor", fY_inter(0), reads=yreads, writes=[(PYk,)])
                s.op("tensor", fY_inter(1), reads=yreads, writes=[(PYk,)])

                def fH(b01):
                    def _f(e, j=j, bp=bp, PAu=PAu, b01=b01, jj=jj, vbase=vbase):
                        for hh in range(2):
                            inst = MM(e,
                                PAu[64 * hh:64 * hh + 64,
                                    128 + 64 * b01: 128 + 64 * b01 + 64],
                                ktT[64 * b01:64 * b01 + 64,
                                    256 * jj + 128 * bp + 64 * hh:
                                    256 * jj + 128 * bp + 64 * hh + 64],
                                VT[64 * b01:64 * b01 + 64,
                                   vbase + 64 * (2 * j + hh): vbase + 64 * (2 * j + hh) + 64],
                                start=(b01 == 0), stop=(b01 == 1),
                                tile_position=(64 * b01, 64 * hh)
                                if (b01 or hh) else None)
                        return inst
                    return _f
                hreads = [("ktT", jj, bp), ("VTb", 4 * bp + 2 * c + j // 4), ("st", us_)]
                s.op("tensor", fH(0), reads=hreads, writes=[(PAk,)])
                s.op("tensor", fH(1), reads=hreads, writes=[(PAk,)])
                ht_ap = htmp[:, 128 * us_:128 * us_ + 128]
                s.op("vector",
                     lambda e, j=j, bp=bp, PAu=PAu, ht_ap=ht_ap: e.tensor_tensor(
                         ht_ap,
                         hst[:, 64 * (4 * j + 2 * bp): 64 * (4 * j + 2 * bp) + 128],
                         PAu[:, 128:256], ALU.add),
                     reads=[(PAk,), ("h", j, 2 * bp), ("h", j, 2 * bp + 1)],
                     writes=[("htmp", us_)])
                for b01 in range(2):
                    b = 2 * bp + b01
                    acc = 8 * j + 2 * b + c
                    s.op("vector",
                         lambda e, j=j, b=b, b01=b01, acc=acc, ht_ap=ht_ap:
                         e.tensor_single_scalar(
                             hst[:, 64 * (4 * j + b): 64 * (4 * j + b) + 64],
                             ht_ap[:, 64 * b01:64 * b01 + 64],
                             acs[:, acc:acc + 1], ALU.mult),
                         reads=[("htmp", us_), ("ac", j)], writes=[("h", j, b)])
                s.op("scalar",
                     lambda e, j=j, bp=bp, base=base, PYu=PYu: e.copy(
                         scE[:, base + 128 * bp: base + 128 * bp + 128],
                         PYu[:, 0:128]),
                     reads=[(PYk,)], writes=[("u", j)])

            emit_ktT(0, 0)
            emit_ktT(0, 1)
            emit_ktT(0, 2)
            emit_front(0)
            emit_front(1)
            emit_front(2)
            YMAP = [0, 4, 1, 5, 2, 6, 3, 7]

            def yln_sq_chase(nb):
                if nb >= 17 and (nb - 17) % 2 == 0:
                    jy = (nb - 17) // 2
                    if jy <= 5:
                        ln_sq_j(jy, [("u", jy)], scE, eng="gpsimd", blk=YMAP[jy])

            for n in range(3, len(units)):
                emit_back(n - 3)
                yln_sq_chase(n - 3)
                if n % 2 == 0:
                    cc, jn, _ = units[n]
                    jn2 = jn + 1
                    cn = cc
                    if jn2 >= NJ:
                        jn2 -= NJ
                        cn += 1
                    if cn < 2:
                        emit_ktT(cn, jn2)
                emit_front(n)
            for nb in (len(units) - 3, len(units) - 2, len(units) - 1):
                emit_back(nb)
                yln_sq_chase(nb)

            for jy in (6, 7):
                ln_sq_j(jy, [("u", jy)], scE, eng=("gpsimd" if jy % 2 else "vector"),
                        blk=YMAP[jy])
            for j in range(NJ):
                ln_mm_j(j, "u", scE, blk=YMAP[j])
            emit_rsY = ln_finish("u", scE, done_mms=NJ, late_rs=True)

            xcw, xbw = XC[0], XB[0]

            def evac_o(m, pap, pkey):
                s.op("vector",
                     lambda e, m=m, pap=pap: e.tensor_tensor(
                         sgscr[:, 0:512], pap, lnb[:, 512:1024], ALU.mult),
                     reads=[(pkey,), ("lnb_rs",)],
                     writes=[("sgs", 0), ("stB",)])
                s.op("vector",
                     lambda e, m=m, xcw=xcw: e.scalar_tensor_tensor(
                         xcw[:, TW * m:TW * m + TW], sgscr[:, 0:512],
                         wob[:, m:m + 1],
                         xcw[:, TW * m:TW * m + TW], ALU.add, ALU.add),
                     reads=[("sgs", 0), ("x", xbw, m), ("wob",)],
                     writes=[("x", xbw, m)])
                if m >= 1:
                    shadow_j(m - 1)
                    ln_sq_j(m - 1, [("x", xbw, m - 1)], xcw, eng="gpsimd")
                if m >= 3:
                    ln_mm_j(m - 3, "xb", xb)
            def wo_nm_step(e, P, m):
                return MM(e, P[:, 0:TW], wsumb[0:1, 128 * m:128 * m + 128],
                          ems[0:1, 0:512], start=False, stop=True)
            proj("wo", "u", scE, evac_o, chase=True, mid=emit_rsY, mid_at=0,
                 mid_pre=True, extra_k=wo_nm_step,
                 extra_reads=[("wsum",), ("em", 0)])
            if g + 1 < ngroups:
                s.dma("scalar_q",
                      lambda e: e.dma_start(out=sb["W_wg"][:], in_=d_wg_s),
                      writes=[("W", "wg")])
            shadow_j(NJ - 1)
            ln_sq_j(NJ - 1, [("x", xbw, NJ - 1)], xcw)
            ln_mm_j(NJ - 3, "xb", xb)

            ln_finish("xb", xb, done_mms=NJ - 2)
            ffn(c3, "c3", tail_stats=False, tail_io=g)

        s.emit(block, limit=limit)
    return nc


def make_host_consts():
    idf = np.eye(128, dtype=np.float32)
    m64 = np.triu(np.ones((CH, CH), dtype=np.float32))
    maskf = np.block([[m64, m64], [m64, m64]]).astype(np.float32)
    return idf, maskf


def host_pre(xc):
    v = xc.reshape(B_PER_CORE, NGROUPS_FULL, 2, CH, NJ, 128)
    v = v.transpose(1, 4, 5, 2, 0, 3)
    return np.ascontiguousarray(v.reshape(NGROUPS_FULL, NJ, 128, TPG))


def host_post(oc):
    v = oc.reshape(NGROUPS_FULL, NJ, 128, 2, B_PER_CORE, CH)
    v = v.transpose(4, 0, 3, 5, 1, 2)
    return np.ascontiguousarray(v.reshape(B_PER_CORE, T, D))


def make_in_maps(x, wT, vecs):
    idf, maskf = make_host_consts()
    in_maps = []
    for c in range(NCORES):
        m = {"x": host_pre(x[B_PER_CORE * c:B_PER_CORE * (c + 1)]),
             "vecs": vecs, "idf": idf, "maskf": maskf}
        for n in W_NAMES:
            m[n] = wT[n]
        in_maps.append(m)
    return in_maps


def prep_inputs(L0_attn_x, L0_attn_Wq, L0_attn_Wk, L0_attn_Wv, L0_attn_Wg, L0_attn_Wo,
                L0_attn_gamma, L0_attn_beta, L0_ffn_W_r, L0_ffn_W_k, L0_ffn_W_v,
                L0_ffn_mu_r, L0_ffn_mu_k):
    x = np.ascontiguousarray(np.asarray(L0_attn_x, dtype=np.float32))
    wmap = {
        "wr": L0_ffn_W_r, "wkf": L0_ffn_W_k, "wvf": L0_ffn_W_v,
        "wq": L0_attn_Wq, "wk": L0_attn_Wk, "wv": L0_attn_Wv,
        "wg": L0_attn_Wg, "wo": L0_attn_Wo,
    }
    wT = {n: np.ascontiguousarray(np.asarray(w, np.float32).T) for n, w in wmap.items()}
    vecs = np.zeros((128, 64), dtype=np.float32)
    vecs[:, 0:8] = np.asarray(L0_ffn_mu_r, np.float32).reshape(8, 128).T
    vecs[:, 8:16] = np.asarray(L0_ffn_mu_k, np.float32).reshape(8, 128).T
    vecs[:, 16:24] = np.asarray(L0_attn_gamma, np.float32).reshape(8, 128).T
    vecs[:, 24:32] = np.asarray(L0_attn_beta, np.float32).reshape(8, 128).T
    return x, wT, vecs


def kernel(**inputs):
    x, wT, vecs = prep_inputs(**inputs)
    nc = build_program(NGROUPS_FULL)
    in_maps = make_in_maps(x, wT, vecs)
    res = run_bass_kernel_spmd(nc, in_maps, core_ids=list(range(NCORES)))
    return np.concatenate([host_post(np.asarray(r["o"], np.float32))
                           for r in res.results], axis=0)



